# version 02
# baseline (speedup 1.0000x reference)
import numpy as np
import ml_dtypes
import concourse.bass as bass
import concourse.mybir as mybir
from concourse.bass_utils import run_bass_kernel_spmd

F32 = mybir.dt.float32
BF16 = mybir.dt.bfloat16
AF = mybir.ActivationFunctionType
ALU = mybir.AluOpType
AX = mybir.AxisListType
bf = ml_dtypes.bfloat16

D = 2048
DFF = 8192
T = 1024
NCH = 16
PAIRS = 16
SCALE = float(1.0 / np.sqrt(32.0))
BIG = float(2.0 ** 63)
HALFBIG = float(0.5 * 2.0 ** 63)
NMAT = float(DFF * D)
GROUPS_KV = [[0, 1, 2, 3], [4, 5, 6, 7]]
GROUP_ALL = [[0, 1, 2, 3, 4, 5, 6, 7]]

DEBUG = False

SEM = {}
NSQ = 8
NGQ = 4
SQ = tuple(f"sq{i}" for i in range(NSQ))
GQ = tuple(f"gq{i}" for i in range(NGQ))
SEMNAMES = SQ + GQ + ("mm", "act", "vec", "gp",
                      "ccr", "cck", "ccv", "cw1", "cw2")


class Plan:

    ENGS = ("sync", "gpsimd", "tensor", "vector", "scalar")

    def __init__(self):
        self.items = {e: [] for e in self.ENGS}
        self.ctr = {s: 0 for s in SEMNAMES}
        self.rr = {"sync": 0, "gpsimd": 0}

    def add(self, eng, fn):
        self.items[eng].append(fn)

    def wait(self, eng, sem, val=None):
        if val is None:
            tok = sem
            if tok is None:
                return
            if isinstance(tok, list):
                best = {}
                for s, v in tok:
                    if s is not None and v > best.get(s, 0):
                        best[s] = v
                for s, v in best.items():
                    self.add(eng, lambda e, s=s, v=v: e.wait_ge(SEM[s], v))
                return
            sem, val = tok
        if sem is None or val is None or val <= 0:
            return
        self.add(eng, lambda e, s=sem, v=val: e.wait_ge(SEM[s], v))

    def dma(self, eng, out_ap, in_ap, **kw):
        pool = SQ if eng == "sync" else GQ
        s = pool[self.rr[eng] % len(pool)]
        self.rr[eng] += 1
        self.ctr[s] += 16
        v = self.ctr[s]
        self.add(eng, lambda e, o=out_ap, i=in_ap, kw=dict(kw), s=s: e.dma_start(
            out=o, in_=i, **kw).then_inc(SEM[s], 16))
        return (s, v)

    def op(self, eng, fn, inc=None):
        if inc is None:
            self.add(eng, fn)
            return None
        self.ctr[inc] += 1
        v = self.ctr[inc]
        self.add(eng, lambda e, f=fn, s=inc: f(e).then_inc(SEM[s], 1))
        return v

    def collective(self, sem, kind, op_, groups, in_ap, out_ap):
        self.ctr[sem] += 1
        v = self.ctr[sem]
        self.add("gpsimd", lambda e, k=kind, o=op_, g=groups, i=in_ap, oo=out_ap,
                 s=sem: e.collective_compute(
                     k, o, replica_groups=g, ins=[i.opt()],
                     outs=[oo.opt()]).then_inc(SEM[s], 1))
        return v


def build_program():
    nc = bass.Bass("TRN2", target_bir_lowering=False)

    x_d = nc.dram_tensor("x", [T, D], F32, kind="ExternalInput").ap()
    wq_d = nc.dram_tensor("wq", [D, D], F32, kind="ExternalInput").ap()
    wk_d = nc.dram_tensor("wk", [D, D], F32, kind="ExternalInput").ap()
    wv_d = nc.dram_tensor("wv", [D, D], F32, kind="ExternalInput").ap()
    wo_d = nc.dram_tensor("wo", [D, D], F32, kind="ExternalInput").ap()
    bq_d = nc.dram_tensor("bq", [128, 16], F32, kind="ExternalInput").ap()
    bk_d = nc.dram_tensor("bk", [128, 16], F32, kind="ExternalInput").ap()
    bo_d = nc.dram_tensor("bo", [128, 16], F32, kind="ExternalInput").ap()
    bv_row_d = nc.dram_tensor("bv_row", [1, D], F32, kind="ExternalInput").ap()
    b1_d = nc.dram_tensor("b1", [128, 64], F32, kind="ExternalInput").ap()
    b2_d = nc.dram_tensor("b2", [128, 16], F32, kind="ExternalInput").ap()
    w1s_d = nc.dram_tensor("w1s", [DFF, 256], F32, kind="ExternalInput").ap()
    w2s_d = nc.dram_tensor("w2s", [D, 1024], F32, kind="ExternalInput").ap()
    out_d = nc.dram_tensor("out", [T, D], F32, kind="ExternalOutput").ap()
    if DEBUG:
        o_xn = nc.dram_tensor("o_xn", [128, NCH * T], BF16,
                              kind="ExternalOutput").ap()
        o_st = nc.dram_tensor("o_st", [128, NCH * T], BF16,
                              kind="ExternalOutput").ap()
        o_fe = nc.dram_tensor("o_fe", [128, NCH * T], BF16,
                              kind="ExternalOutput").ap()
        o_h = nc.dram_tensor("o_h", [128, 32768], BF16,
                             kind="ExternalOutput").ap()
        o_kg = nc.dram_tensor("o_kg", [4 * D, T], BF16,
                              kind="ExternalOutput").ap()
        o_vg = nc.dram_tensor("o_vg", [4 * T, 2080], BF16,
                              kind="ExternalOutput").ap()
        o_w1t = nc.dram_tensor("o_w1t", [D, DFF], BF16,
                               kind="ExternalOutput").ap()
        o_w2t = nc.dram_tensor("o_w2t", [DFF, D], BF16,
                               kind="ExternalOutput").ap()
        o_sum = nc.dram_tensor("o_sum", [1, 2], F32, kind="ExternalOutput").ap()
        o_q = nc.dram_tensor("o_q", [128, NCH * T], BF16,
                             kind="ExternalOutput").ap()

    d_k_in = nc.dram_tensor("d_k_in", [D, T], BF16).ap()
    d_k = nc.dram_tensor("d_k", [4 * D, T], BF16).ap()
    d_v_in = nc.dram_tensor("d_v_in", [T, 2080], BF16).ap()
    d_v = nc.dram_tensor("d_v", [4 * T, 2080], BF16).ap()
    d_w1t_in = nc.dram_tensor("d_w1t_in", [256, DFF], BF16).ap()
    d_w1t = nc.dram_tensor("d_w1t", [D, DFF], BF16).ap()
    d_w2t_in = nc.dram_tensor("d_w2t_in", [1024, D], BF16).ap()
    d_w2t = nc.dram_tensor("d_w2t", [DFF, D], BF16).ap()
    d_sum_in = nc.dram_tensor("d_sum_in", [1, 2], F32).ap()
    d_sum = nc.dram_tensor("d_sum", [1, 2], F32).ap()

    from contextlib import ExitStack
    ctx = ExitStack()
    with ctx:
        def sbt(name, shape, dt):
            return ctx.enter_context(nc.sbuf_tensor(name, shape, dt))

        def pst(name, shape, dt):
            return ctx.enter_context(nc.psum_tensor(name, shape, dt))

        xnbuf = sbt("xnbuf", [128, NCH * T], BF16)
        qbuf = sbt("qbuf", [128, NCH * T], BF16)
        hbuf = sbt("hbuf", [128, 37120], BF16)
        wbufA = sbt("wbufA", [128, 8192], BF16)
        wbufB = sbt("wbufB", [128, 8192], BF16)
        fa = sbt("fa", [128, 2048], F32)
        fb = sbt("fb", [128, 2048], F32)
        fout = sbt("fout", [128, 1024], F32)
        s_bq = sbt("s_bq", [128, 16], F32)
        s_bk = sbt("s_bk", [128, 16], F32)
        s_bo = sbt("s_bo", [128, 16], F32)
        s_b1 = sbt("s_b1", [128, 64], F32)
        s_b2 = sbt("s_b2", [128, 16], F32)
        s_bvr = sbt("s_bvr", [1, 2048], F32)
        ident = sbt("ident", [128, 128], BF16)
        ones1 = sbt("ones1", [1, 128], F32)
        ones128 = sbt("ones128", [128, 1], F32)
        ones64f = sbt("ones64f", [1, 64], F32)
        stat = sbt("stat", [128, 8], F32)
        epst = sbt("epst", [128, 1], F32)
        acc = sbt("acc", [128, 2], F32)
        acct = sbt("acct", [128, 1], F32)
        ssmall = sbt("ssmall", [1, 8], F32)
        inv_bc = sbt("inv_bc", [128, 4], F32)
        recA = sbt("recA", [1, 512], F32)
        recB = sbt("recB", [1, 512], F32)
        bcbufA = sbt("bcbufA", [64, 512], F32)
        bcbufB = sbt("bcbufB", [64, 512], F32)

        H_XN = 0
        H_KT = 0
        H_VST = 2048
        H_TU, H_TV, H_TT = 0, 2048, 4096
        H_WST = 6144
        H_KB = 0
        H_VB = 8192
        H_EA = 16512
        H_EB = 18560
        H_FE = 20608
        H_H = 0
        H_PBF = 32768

        P_a = pst("P_a", [128, 1024], F32)
        P_b = pst("P_b", [128, 1024], F32)
        P_c = pst("P_c", [128, 512], F32)
        P_d = pst("P_d", [128, 512], F32)
        P_f = pst("P_f", [128, 2048], BF16)

        SEM.clear()
        for s in SEMNAMES:
            SEM[s] = ctx.enter_context(nc.semaphore(s))

        P = Plan()
        C = P.ctr

        P.op("gpsimd", lambda e: e.memset(ident[:, :], 0.0))
        v_id = P.op("gpsimd", lambda e: e.affine_select(
            out=ident[:, :], in_=ident[:, :], compare_op=ALU.not_equal,
            fill=1.0, base=0, pattern=[[-1, 128]], channel_multiplier=1),
            inc="gp")
        P.op("gpsimd", lambda e: e.memset(ones1[:, :], 1.0))
        P.op("gpsimd", lambda e: e.memset(ones128[:, :], 1.0))
        P.op("gpsimd", lambda e: e.memset(ones64f[:, :], 1.0))
        v_ones = P.op("gpsimd", lambda e: e.memset(epst[:, :], 1e-5), inc="gp")

        toks_bias = [P.dma("sync", s_bq[:, :], bq_d[:, :]),
                     P.dma("sync", s_bk[:, :], bk_d[:, :]),
                     P.dma("sync", s_bo[:, :], bo_d[:, :]),
                     P.dma("sync", s_b1[:, :], b1_d[:, :]),
                     P.dma("sync", s_b2[:, :], b2_d[:, :])]
        v_bias = P.dma("sync", s_bvr[:, :], bv_row_d[:, :])

        v_acc0 = P.op("vector", lambda e: e.memset(acc[:, :], 0.0), inc="vec")
        fa_free = ("vec", 0)
        for which, src_ap, ntile, tw in ((0, w1s_d, 64, 256), (1, w2s_d, 16, 1024)):
            for k in range(ntile):
                P.wait("sync", fa_free[0], fa_free[1])
                vl = P.dma("sync", fa[:, 0:tw], src_ap[k * 128:(k + 1) * 128, :])
                P.wait("vector", vl)
                P.op("vector", lambda e, tw=tw: e.reduce_sum(
                    acct[:, 0:1], fa[:, 0:tw], axis=AX.X,
                    apply_absolute_value=True), inc="vec")
                fa_free = ("vec", C["vec"])
                P.wait("vector", "vec", C["vec"])
                P.op("vector", lambda e, w=which: e.tensor_add(
                    acc[:, w:w + 1], acc[:, w:w + 1], acct[:, 0:1]), inc="vec")
        v_abs_done = C["vec"]
        P.wait("tensor", "vec", v_abs_done)
        P.wait("tensor", "gp", v_ones)
        v_sum_mm = P.op("tensor", lambda e: e.matmul(
            P_c[0:1, 0:2], ones128[:, 0:1], acc[:, :], start=True, stop=True),
            inc="mm")
        P.wait("vector", "mm", v_sum_mm)
        v_sum_cp = P.op("vector", lambda e: e.tensor_copy(
            ssmall[0:1, 0:2], P_c[0:1, 0:2]), inc="vec")
        P.wait("sync", "vec", v_sum_cp)
        v_sum_st = P.dma("sync", d_sum_in[:, :], ssmall[0:1, 0:2])
        P.wait("gpsimd", v_sum_st)
        v_ar = P.collective("ccr", "AllReduce", ALU.add, GROUP_ALL, d_sum_in, d_sum)

        xnbf = hbuf[:, H_XN:H_XN + 2048]
        xnbf_free = ("mm", 0)
        pf_evac = {0: 0, 1: 0}
        v_tph = 0
        for i in range(8):
            P.wait("sync", fa_free[0], fa_free[1])
            vl = P.dma("sync", fa[:, :], x_d[i * 128:(i + 1) * 128, :])
            P.wait("vector", vl)
            P.op("vector", lambda e: e.reduce_sum(
                stat[:, 0:1], fa[:, :], axis=AX.X, negate=True), inc="vec")
            P.wait("vector", "vec", C["vec"])
            v_nm = P.op("vector", lambda e: e.tensor_scalar_mul(
                stat[:, 1:2], stat[:, 0:1], 1.0 / 2048.0), inc="vec")
            P.wait("scalar", "vec", v_nm)
            P.op("scalar", lambda e: e.activation(
                out=fb[:, :], in_=fa[:, :], func=AF.Identity,
                bias=stat[:, 1:2], scale=1.0), inc="act")
            P.wait("scalar", "act", C["act"])
            v_sqr = P.op("scalar", lambda e: e.activation(
                out=fa[:, :], in_=fb[:, :], func=AF.Square,
                accum_out=stat[:, 2:3]), inc="act")
            fa_free = ("act", v_sqr)
            P.wait("scalar", "act", C["act"])
            if i == 0:
                P.wait("scalar", "gp", v_ones)
            v_std = P.op("scalar", lambda e: e.activation(
                out=stat[:, 3:4], in_=stat[:, 2:3], func=AF.Sqrt,
                bias=epst[:, 0:1], scale=1.0 / 2048.0), inc="act")
            P.wait("vector", "act", v_std)
            v_rstd = P.op("vector", lambda e: e.reciprocal(
                stat[:, 4:5], stat[:, 3:4]), inc="vec")
            P.wait("vector", "vec", C["vec"])
            v_rstd2 = P.op("vector", lambda e: e.tensor_copy(
                stat[:, 5:6], stat[:, 4:5]), inc="vec")
            P.wait("scalar", "vec", v_rstd2)
            P.wait("scalar", xnbf_free[0], xnbf_free[1])
            v_xn = P.op("scalar", lambda e: e.activation(
                out=xnbf[:, :], in_=fb[:, :], func=AF.Identity,
                scale=stat[:, 5:6]), inc="act")
            P.wait("tensor", "act", v_xn)
            if i == 0:
                P.wait("tensor", "gp", v_id)
            for half in range(2):
                P.wait("tensor", "vec", pf_evac[half])
                for k in range(8):
                    kk = half * 8 + k
                    fn = lambda e, kk=kk, k=k, half=half: e.transpose(
                        P_f[:, half * 1024 + k * 128: half * 1024 + k * 128 + 128],
                        xnbf[:, kk * 128:kk * 128 + 128], ident[:, :])
                    if k == 7:
                        v_tph = P.op("tensor", fn, inc="mm")
                    else:
                        P.op("tensor", fn)
                P.wait("vector", "mm", v_tph)
                out3 = xnbuf[:, half * 8 * T: (half * 8 + 8) * T].rearrange(
                    "p (c t) -> p c t", t=T)[:, :, i * 128:i * 128 + 128]
                in3 = P_f[:, half * 1024: half * 1024 + 1024].rearrange(
                    "p (c t) -> p c t", t=128)
                pf_evac[half] = P.op(
                    "vector", lambda e, o=out3, ii=in3: e.tensor_copy(o, ii),
                    inc="vec")
            xnbf_free = ("mm", v_tph)
        v_xnT_done = max(pf_evac.values())

        def band_ap(w_ap, jd):
            return w_ap[:, jd * 128:(jd + 1) * 128].rearrange(
                "(i p) d -> p i d", p=128)

        def chunked(sb_ap, inner):
            return sb_ap.rearrange("p (i d) -> p i d", d=inner)

        band_free = {0: ("gp", 0), 1: ("gp", 0)}
        psum_free = {0: ("act", 0), 1: ("act", 0)}
        kst_free = {0: None, 1: None}
        toks_kst = []
        v_kmm_last = 0
        for jd in range(16):
            p = jd % 2
            wb = (wbufA if p == 0 else wbufB)[:, 0:2048]
            P.wait("gpsimd", band_free[p][0], band_free[p][1])
            vw = P.dma("gpsimd", chunked(wb, 128), band_ap(wk_d, jd))
            P.wait("tensor", vw)
            P.wait("tensor", psum_free[p][0], psum_free[p][1])
            if jd == 0:
                P.wait("tensor", "vec", v_xnT_done)
            P_x = P_a if p == 0 else P_b
            for hh in range(2):
                for i in range(16):
                    fn = lambda e, wb=wb, i=i, hh=hh, P_x=P_x: e.matmul(
                        P_x[:, hh * 512:(hh + 1) * 512],
                        wb[:, i * 128:(i + 1) * 128],
                        xnbuf[:, i * T + hh * 512: i * T + (hh + 1) * 512],
                        start=(i == 0), stop=(i == 15))
                    if hh == 1 and i == 15:
                        v_kmm = P.op("tensor", fn, inc="mm")
                    else:
                        P.op("tensor", fn)
            band_free[p] = ("mm", v_kmm)
            v_kmm_last = v_kmm
            P.wait("scalar", "mm", v_kmm)
            P.wait("scalar", kst_free[p])
            if jd == 0:
                P.wait("scalar", toks_bias)
            if jd < 2:
                P.wait("scalar", "mm", v_tph)
            kts = hbuf[:, H_KT + p * 1024: H_KT + (p + 1) * 1024]
            v_kev = P.op("scalar", lambda e, P_x=P_x, jd=jd, kts=kts: e.activation(
                out=kts, in_=P_x[:, :], func=AF.Identity,
                bias=s_bk[:, jd:jd + 1], scale=1.0), inc="act")
            psum_free[p] = ("act", v_kev)
            P.wait("sync", "act", v_kev)
            v_kst = P.dma("sync", d_k_in[jd * 128:(jd + 1) * 128, :], kts)
            toks_kst.append(v_kst)
            kst_free[p] = v_kst
        P.wait("gpsimd", toks_kst)
        v_agk = P.collective("cck", "AllGather", ALU.bypass, GROUPS_KV, d_k_in, d_k)

        P.wait("tensor", v_bias)
        P.wait("tensor", "gp", v_ones)
        P.wait("tensor", "vec", v_sum_cp)
        v_bvbc = 0
        for q in range(4):
            vq = P.op("tensor", lambda e, q=q: e.matmul(
                P_c[:, :], ones1[0:1, :], s_bvr[0:1, q * 512:(q + 1) * 512],
                start=True, stop=True), inc="mm")
            P.wait("vector", "mm", vq)
            v_bvbc = P.op("vector", lambda e, q=q: e.tensor_copy(
                fa[:, q * 512:(q + 1) * 512], P_c[:, :]), inc="vec")
            P.wait("tensor", "vec", v_bvbc)

        P.wait("gpsimd", "vec", v_xnT_done)
        v_vst_ms = 0
        for tt in range(8):
            v_vst_ms = P.op("gpsimd", lambda e, tt=tt: e.memset(
                hbuf[:, H_VST + tt * 2080: H_VST + (tt + 1) * 2080], 1.0),
                inc="gp")
        v_vev = {}
        vband_free = {0: band_free[0], 1: band_free[1]}
        vpsum_free = {0: ("vec", v_bvbc), 1: ("vec", v_bvbc)}
        for q in range(4):
            p = q % 2
            wb = wbufA if p == 0 else wbufB
            P.wait("gpsimd", vband_free[p][0], vband_free[p][1])
            vw = P.dma("gpsimd", chunked(wb[:, :], 512),
                       wv_d[:, q * 512:(q + 1) * 512].rearrange(
                           "(i p) d -> p i d", p=128))
            v_vmm = 0
            for tt in range(8):
                pp = tt % 2
                P_x = P_c if pp == 0 else P_d
                P.wait("tensor", vw)
                P.wait("tensor", vpsum_free[pp][0], vpsum_free[pp][1])
                for i in range(16):
                    fn = lambda e, i=i, wb=wb, tt=tt, P_x=P_x: e.matmul(
                        P_x[:, :],
                        xnbuf[:, i * T + tt * 128: i * T + tt * 128 + 128],
                        wb[:, i * 512:(i + 1) * 512],
                        start=(i == 0), stop=(i == 15))
                    if i == 15:
                        v_vmm = P.op("tensor", fn, inc="mm")
                    else:
                        P.op("tensor", fn)
                P.wait("vector", "mm", v_vmm)
                if q == 0 and tt == 0:
                    P.wait("vector", "gp", v_vst_ms)
                base = H_VST + tt * 2080 + q * 520
                o3 = hbuf[:, base: base + 520].rearrange(
                    "p (h e) -> p h e", e=65)[:, :, 0:64]
                i3 = P_x[:, :].rearrange("p (h e) -> p h e", e=64)
                b3 = fa[:, q * 512:(q + 1) * 512].rearrange(
                    "p (h e) -> p h e", e=64)
                vv = P.op("vector", lambda e, o3=o3, i3=i3, b3=b3: e.tensor_tensor(
                    out=o3, in0=i3, in1=b3, op=ALU.add), inc="vec")
                v_vev[(tt, q)] = vv
                vpsum_free[pp] = ("vec", vv)
            vband_free[p] = ("mm", v_vmm)
        toks_vst = []
        for tt in range(8):
            P.wait("sync", "vec", v_vev[(tt, 3)])
            toks_vst.append(P.dma(
                "sync", d_v_in[tt * 128:(tt + 1) * 128, :],
                hbuf[:, H_VST + tt * 2080: H_VST + (tt + 1) * 2080]))
        P.wait("gpsimd", toks_vst)
        v_agv = P.collective("ccv", "AllGather", ALU.bypass, GROUPS_KV, d_v_in, d_v)

        qband_free = {0: vband_free[0], 1: vband_free[1]}
        qpsum_free = {0: psum_free[0], 1: psum_free[1]}
        v_qev = 0
        for jd in range(16):
            p = jd % 2
            wb = (wbufA if p == 0 else wbufB)[:, 0:2048]
            P.wait("gpsimd", qband_free[p][0], qband_free[p][1])
            vw = P.dma("gpsimd", chunked(wb, 128), band_ap(wq_d, jd))
            P.wait("tensor", vw)
            P.wait("tensor", qpsum_free[p][0], qpsum_free[p][1])
            P_x = P_a if p == 0 else P_b
            for hh in range(2):
                for i in range(16):
                    fn = lambda e, wb=wb, i=i, hh=hh, P_x=P_x: e.matmul(
                        P_x[:, hh * 512:(hh + 1) * 512],
                        wb[:, i * 128:(i + 1) * 128],
                        xnbuf[:, i * T + hh * 512: i * T + (hh + 1) * 512],
                        start=(i == 0), stop=(i == 15))
                    if hh == 1 and i == 15:
                        v_qmm = P.op("tensor", fn, inc="mm")
                    else:
                        P.op("tensor", fn)
            qband_free[p] = ("mm", v_qmm)
            P.wait("scalar", "mm", v_qmm)
            v_qev = P.op("scalar", lambda e, P_x=P_x, jd=jd: e.activation(
                out=qbuf[:, jd * T:(jd + 1) * T], in_=P_x[:, :],
                func=AF.Identity, bias=s_bq[:, jd:jd + 1], scale=1.0), inc="act")
            qpsum_free[p] = ("act", v_qev)

        P.wait("sync", "ccr", v_ar)
        v_sumld = P.dma("sync", ssmall[0:1, 2:4], d_sum[:, :])
        P.wait("vector", v_sumld)
        P.op("vector", lambda e: e.tensor_scalar(
            out=ssmall[0:1, 4:6], in0=ssmall[0:1, 2:4], scalar1=1.0 / NMAT,
            scalar2=1e-8, op0=ALU.mult, op1=ALU.max), inc="vec")
        P.wait("vector", "vec", C["vec"])
        v_inv = P.op("vector", lambda e: e.reciprocal(
            ssmall[0:1, 6:8], ssmall[0:1, 4:6]), inc="vec")
        P.wait("vector", "vec", C["vec"])
        P.wait("tensor", "vec", v_inv)
        P.wait("tensor", vpsum_free[1][0], vpsum_free[1][1])
        v_invb = P.op("tensor", lambda e: e.matmul(
            P_d[:, 0:2], ones1[0:1, :], ssmall[0:1, 6:8],
            start=True, stop=True), inc="mm")
        P.wait("vector", "mm", v_invb)
        P.op("vector", lambda e: e.tensor_scalar_mul(
            inv_bc[:, 0:2], P_d[:, 0:2], BIG), inc="vec")
        v_invbc = P.op("vector", lambda e: e.tensor_scalar_mul(
            inv_bc[:, 2:4], P_d[:, 0:2], -BIG), inc="vec")

        v_wst = {}
        pf_free = {0: ("vec", pf_evac[0]), 1: ("vec", pf_evac[1])}
        tern_free = ("mm", 0)
        v_w1std = v_w2std = 0
        for which, src_ap, ntile, tw in ((0, w1s_d, 64, 256), (1, w2s_d, 16, 1024)):
            ntp = tw // 128
            v_tp = 0
            for k in range(ntile):
                P.wait("sync", fa_free[0], fa_free[1])
                if which == 0 and k == 0:
                    P.wait("sync", "vec", v_vev[(7, 3)])
                vl = P.dma("sync", fa[:, 0:tw], src_ap[k * 128:(k + 1) * 128, :])
                P.wait("vector", vl)
                P.wait("vector", "vec", v_invbc)
                if which == 0 and k == 0:
                    P.wait("vector", toks_vst)
                    P.wait("vector", toks_kst)
                P.wait("vector", tern_free[0], tern_free[1])
                m = which
                P.op("vector", lambda e, tw=tw, m=m: e.tensor_scalar(
                    out=fb[:, 0:tw], in0=fa[:, 0:tw], scalar1=inv_bc[:, m:m + 1],
                    scalar2=-HALFBIG, op0=ALU.mult, op1=ALU.add), inc="vec")
                P.op("vector", lambda e, tw=tw: e.tensor_scalar(
                    out=hbuf[:, H_TU:H_TU + tw], in0=fb[:, 0:tw], scalar1=0.0,
                    scalar2=1.0, op0=ALU.max, op1=ALU.min), inc="vec")
                v_u2 = P.op("vector", lambda e, tw=tw, m=m: e.tensor_scalar(
                    out=fb[:, 0:tw], in0=fa[:, 0:tw],
                    scalar1=inv_bc[:, 2 + m:3 + m],
                    scalar2=-HALFBIG, op0=ALU.mult, op1=ALU.add), inc="vec")
                fa_free = ("vec", v_u2)
                P.op("vector", lambda e, tw=tw: e.tensor_scalar(
                    out=hbuf[:, H_TV:H_TV + tw], in0=fb[:, 0:tw], scalar1=0.0,
                    scalar2=1.0, op0=ALU.max, op1=ALU.min), inc="vec")
                v_tt = P.op("vector", lambda e, tw=tw: e.tensor_tensor(
                    out=hbuf[:, H_TT:H_TT + tw], in0=hbuf[:, H_TU:H_TU + tw],
                    in1=hbuf[:, H_TV:H_TV + tw], op=ALU.subtract), inc="vec")
                P.wait("tensor", "vec", v_tt)
                half = k % 2
                P.wait("tensor", pf_free[half][0], pf_free[half][1])
                for j in range(ntp):
                    fn = lambda e, j=j, half=half: e.transpose(
                        P_f[:, half * 1024 + j * 128: half * 1024 + (j + 1) * 128],
                        hbuf[:, H_TT + j * 128:H_TT + (j + 1) * 128], ident[:, :])
                    if j == ntp - 1:
                        v_tp = P.op("tensor", fn, inc="mm")
                    else:
                        P.op("tensor", fn)
                tern_free = ("mm", v_tp)
                P.wait("vector", "mm", v_tp)
                if which == 1 and k == 0:
                    P.wait("vector", toks_w1st)
                cw = 8192 if which == 0 else 2048
                o3 = hbuf[:, H_WST:H_WST + ntp * cw].rearrange(
                    "p (j c) -> p j c", c=cw)[:, :, k * 128:k * 128 + 128]
                i3 = P_f[:, half * 1024: half * 1024 + ntp * 128].rearrange(
                    "p (j c) -> p j c", c=128)
                pf_free[half] = ("vec", P.op(
                    "vector", lambda e, o3=o3, i3=i3: e.tensor_copy(o3, i3),
                    inc="vec"))
            nrow = ntp
            cw = 8192 if which == 0 else 2048
            dst = d_w1t_in if which == 0 else d_w2t_in
            toks_std = []
            for j in range(nrow):
                P.wait("sync", "vec", pf_free[(ntile - 1) % 2][1])
                P.wait("sync", "vec", pf_free[(ntile - 2) % 2][1])
                toks_std.append(P.dma(
                    "sync", dst[j * 128:(j + 1) * 128, :],
                    hbuf[:, H_WST + j * cw:H_WST + (j + 1) * cw]))
            if which == 0:
                toks_w1st = toks_std
                P.wait("gpsimd", toks_w1st)
                v_agw1 = P.collective("cw1", "AllGather", ALU.bypass, GROUP_ALL,
                                      d_w1t_in, d_w1t)
            else:
                toks_w2st = toks_std
                P.wait("gpsimd", toks_w2st)
                v_agw2 = P.collective("cw2", "AllGather", ALU.bypass, GROUP_ALL,
                                      d_w2t_in, d_w2t)

        kvld = {}
        feats_evac = {}
        v_feats_evac_last = 0
        exp_free = ("mm", 0)

        for j in range(PAIRS):
            p = j % 2
            kb = hbuf[:, H_KB + p * 4096: H_KB + (p + 1) * 4096]
            vb = hbuf[:, H_VB + p * 4160: H_VB + (p + 1) * 4160]
            if j >= 2:
                P.wait("sync", "vec", feats_evac[(j - 2, 1)])
            else:
                P.wait("sync", "cck", v_agk)
                P.wait("sync", "ccv", v_agv)
                P.wait("sync", toks_w2st)
                P.wait("sync", "mm", tern_free[1])
            toks_kv = []
            for r in range(4):
                toks_kv.append(P.dma(
                    "sync", kb[:, r * 1024:(r + 1) * 1024],
                    d_k[r * 2048 + j * 128: r * 2048 + (j + 1) * 128, :]))
            for r in range(4):
                toks_kv.append(P.dma(
                    "sync", chunked(vb[:, r * 1040:(r + 1) * 1040], 130),
                    d_v[r * 1024:(r + 1) * 1024,
                        130 * j:130 * (j + 1)].rearrange(
                            "(c p) m -> p c m", p=128)))
            kvld[j] = toks_kv

            for qt in range(2):
                exp_ms = {}
                mmg = {}
                q0 = qt * 512
                expA = [hbuf[:, H_EA + par * 1024: H_EA + (par + 1) * 1024]
                        for par in range(2)]
                expB = [hbuf[:, H_EB + par * 1024: H_EB + (par + 1) * 1024]
                        for par in range(2)]

                def mm2_group(u, stop):
                    for kc in (2 * u, 2 * u + 1):
                        par = u % 2
                        off = (kc % 2) * 512
                        for (head, Pfe, eb) in ((0, P_c, expA), (1, P_d, expB)):
                            cbase = kc * 130 + head * 65
                            fn = (lambda e, vb=vb, cbase=cbase, Pfe=Pfe,
                                  eb=eb, par=par, off=off, kc=kc: e.matmul(
                                      Pfe[0:65, :],
                                      vb[:, cbase:cbase + 65],
                                      eb[par][:, off:off + 512],
                                      start=(kc == 0), stop=(kc == 31)))
                            yield fn

                for u in range(16):
                    if u == 0:
                        if qt == 0:
                            P.wait("tensor", kvld[j])
                            if j == 0:
                                P.wait("tensor", "act", v_qev)
                            if j >= 2:
                                P.wait("tensor", "vec", feats_evac[(j - 2, 1)])
                        if (j, qt) != (0, 0):
                            prev = (j, 0) if qt == 1 else (j - 1, 1)
                            P.wait("tensor", "vec", feats_evac.get(prev, 0))
                    else:
                        P.wait("tensor", "act", exp_ms[u - 1])
                        for fn in mm2_group(u - 1, stop=False):
                            P.op("tensor", fn)
                    v_g = 0
                    for kc in (2 * u, 2 * u + 1):
                        off = (kc % 2) * 512
                        for head, Psc, rl, rh in ((0, P_a, 0, 64), (1, P_b, 64, 128)):
                            fn = (lambda e, kb=kb, kc=kc, off=off, Psc=Psc,
                                  rl=rl, rh=rh, j=j, q0=q0: e.matmul(
                                      Psc[:, off:off + 512],
                                      kb[rl:rh, kc * 128:(kc + 1) * 128],
                                      qbuf[rl:rh, j * T + q0: j * T + q0 + 512],
                                      start=True, stop=True,
                                      tile_position=(rl, 0)))
                            if kc % 2 == 1 and head == 1:
                                v_g = P.op("tensor", fn, inc="mm")
                            else:
                                P.op("tensor", fn)
                    mmg[u] = v_g
                    P.wait("scalar", "mm", mmg[u])
                    par = u % 2
                    P.op("scalar", lambda e, par=par: e.activation(
                        out=expA[par][:, :], in_=P_a[:, :], func=AF.Exp,
                        scale=SCALE), inc="act")
                    exp_ms[u] = P.op("scalar", lambda e, par=par: e.activation(
                        out=expB[par][:, :], in_=P_b[:, :], func=AF.Exp,
                        scale=SCALE), inc="act")
                P.wait("tensor", "act", exp_ms[15])
                v_feats = 0
                fns = list(mm2_group(15, stop=True))
                for idx, fn in enumerate(fns):
                    if idx == len(fns) - 1:
                        v_feats = P.op("tensor", fn, inc="mm")
                    else:
                        P.op("tensor", fn)
                P.wait("vector", "mm", v_feats)
                P.op("vector", lambda e: e.reciprocal(
                    recA[0:1, :], P_c[64:65, :]), inc="vec")
                P.wait("vector", "vec", C["vec"])
                v_rec = P.op("vector", lambda e: e.reciprocal(
                    recB[0:1, :], P_d[64:65, :]), inc="vec")
                P.wait("vector", "vec", C["vec"])
                P.wait("tensor", "vec", v_rec)
                P.op("tensor", lambda e: e.matmul(
                    P_a[0:64, 0:512], ones64f[0:1, :], recA[0:1, :],
                    start=True, stop=True))
                v_bc = P.op("tensor", lambda e: e.matmul(
                    P_b[0:64, 0:512], ones64f[0:1, :], recB[0:1, :],
                    start=True, stop=True), inc="mm")
                P.wait("vector", "mm", v_bc)
                if j == 0 and qt == 0:
                    P.wait("vector", toks_w2st)
                fcol = j * T + q0
                P.op("vector", lambda e: e.tensor_copy(
                    bcbufA[:, :], P_a[0:64, 0:512]), inc="vec")
                P.op("vector", lambda e, fcol=fcol: e.tensor_tensor(
                    out=hbuf[0:64, H_FE + fcol:H_FE + fcol + 512],
                    in0=P_c[0:64, :], in1=bcbufA[:, :], op=ALU.mult), inc="vec")
                P.op("vector", lambda e: e.tensor_copy(
                    bcbufB[:, :], P_b[0:64, 0:512]), inc="vec")
                vfe = P.op("vector", lambda e, fcol=fcol: e.tensor_tensor(
                    out=hbuf[64:128, H_FE + fcol:H_FE + fcol + 512],
                    in0=P_d[0:64, :], in1=bcbufB[:, :], op=ALU.mult), inc="vec")
                feats_evac[(j, qt)] = vfe
                v_feats_evac_last = vfe

        v_dbg_fe = None
        if DEBUG:
            P.wait("sync", "vec", v_feats_evac_last)
            v_dbg_fe = [P.dma("sync", o_fe[:, :], hbuf[:, H_FE:H_FE + NCH * T]),
                        P.dma("sync", o_q[:, :], qbuf[:, :])]

        wo_band_free = {0: ("mm", 0), 1: ("mm", 0)}
        wo_psum_free = {0: ("vec", v_feats_evac_last), 1: ("vec", v_feats_evac_last)}
        v_woev = {}
        v_womm_last = 0
        for jd in range(16):
            p = jd % 2
            wb = (wbufA if p == 0 else wbufB)[:, 0:2048]
            P.wait("gpsimd", wo_band_free[p][0], wo_band_free[p][1])
            vw = P.dma("gpsimd", chunked(wb, 128), band_ap(wo_d, jd))
            P.wait("tensor", vw)
            P.wait("tensor", wo_psum_free[p][0], wo_psum_free[p][1])
            if jd == 0:
                P.wait("tensor", "vec", v_feats_evac_last)
                if DEBUG:
                    P.wait("tensor", v_dbg_fe)
            P_x = P_a if p == 0 else P_b
            for hh in range(2):
                for i in range(16):
                    fe = H_FE + i * T + hh * 512
                    fn = lambda e, wb=wb, i=i, hh=hh, P_x=P_x, fe=fe: e.matmul(
                        P_x[:, hh * 512:(hh + 1) * 512],
                        wb[:, i * 128:(i + 1) * 128],
                        hbuf[:, fe:fe + 512],
                        start=(i == 0), stop=(i == 15))
                    if hh == 1 and i == 15:
                        v_womm = P.op("tensor", fn, inc="mm")
                    else:
                        P.op("tensor", fn)
            wo_band_free[p] = ("mm", v_womm)
            v_womm_last = v_womm
            P.wait("scalar", "mm", v_womm)
            vv = P.op("scalar", lambda e, P_x=P_x, jd=jd: e.activation(
                out=qbuf[:, jd * T:(jd + 1) * T], in_=P_x[:, :],
                func=AF.Identity, bias=s_bo[:, jd:jd + 1], scale=1.0), inc="act")
            v_woev[jd] = vv
            wo_psum_free[p] = ("act", vv)
        v_woev_last = v_woev[15]

        w1_free = {0: ("mm", 0), 1: ("mm", 0)}
        h_psum_free = {0: ("act", 0), 1: ("act", 0)}
        f2_psum_free = ("act", 0)
        v_fb_last = C["vec"]
        fa_slot_free = {sl: ("vec", v_fb_last) for sl in range(4)}
        fb_slot_free = {sl: ("vec", v_fb_last) for sl in range(4)}
        pbf_free = {0: ("mm", 0), 1: ("mm", 0)}
        pf2_free = {0: pf_free[0], 1: pf_free[1]}
        fout_free = {0: None, 1: None}
        w2_free = {0: ("mm", 0), 1: ("mm", 0)}
        v_out_dma = 0
        first_gelu = True
        for th in range(2):
            t0 = th * 512
            v_hmm = {}
            v_gelu = {}
            for sb_ in range(32):
                p = sb_ % 2
                wb = (wbufA if p == 0 else wbufB)[:, 0:4096]
                P.wait("sync", w1_free[p][0], w1_free[p][1])
                if th == 0 and sb_ < 2:
                    P.wait("sync", "cw1", 1)
                    P.wait("sync", "mm", v_womm_last)
                toks_w1l = []
                for i in range(16):
                    toks_w1l.append(P.dma(
                        "sync", wb[:, i * 256:(i + 1) * 256],
                        d_w1t[i * 128:(i + 1) * 128,
                              sb_ * 256:(sb_ + 1) * 256]))
                for l in range(2):
                    bb = sb_ * 2 + l
                    pp = bb % 2
                    P_x = P_c if pp == 0 else P_d
                    P.wait("tensor", toks_w1l)
                    P.wait("tensor", h_psum_free[pp][0], h_psum_free[pp][1])
                    if th == 0 and bb == 0:
                        P.wait("tensor", "vec", v_feats_evac_last)
                    for i in range(16):
                        fn = (lambda e, wb=wb, i=i, l=l, P_x=P_x, t0=t0: e.matmul(
                            P_x[:, :],
                            wb[:, i * 256 + l * 128: i * 256 + (l + 1) * 128],
                            xnbuf[:, i * T + t0: i * T + t0 + 512],
                            start=(i == 0), stop=(i == 15)))
                        if i == 15:
                            v_hmm[bb] = P.op("tensor", fn, inc="mm")
                        else:
                            P.op("tensor", fn)
                    P.wait("scalar", "mm", v_hmm[bb])
                    if DEBUG and th == 1 and bb == 0:
                        P.wait("scalar", v_dbg_h)
                    if first_gelu:
                        P.wait("scalar", "mm", v_womm_last)
                        first_gelu = False
                    v_gelu[bb] = P.op("scalar", lambda e, P_x=P_x, bb=bb: e.activation(
                        out=hbuf[:, H_H + bb * 512:H_H + (bb + 1) * 512],
                        in_=P_x[:, :], func=AF.Gelu, bias=s_b1[:, bb:bb + 1],
                        scale=1.0), inc="act")
                    h_psum_free[pp] = ("act", v_gelu[bb])
                w1_free[p] = ("mm", v_hmm[sb_ * 2 + 1])
            v_h_done = v_gelu[63]
            if DEBUG and th == 0:
                P.wait("sync", "act", v_h_done)
                v_dbg_h = P.dma("sync", o_h[:, :], hbuf[:, H_H:H_H + 32768])
            f2slots = [P_a[:, 0:512], P_a[:, 512:1024],
                       P_b[:, 0:512], P_b[:, 512:1024]]
            for g in range(4):
                v_f2 = {}
                for fbk in range(64):
                    p = fbk % 2
                    w2b = (wbufA if p == 0 else wbufB)[:, 4096:4608]
                    P.wait("sync", w2_free[p][0], w2_free[p][1])
                    if th == 0 and g == 0 and fbk < 2:
                        P.wait("sync", "cw2", 1)
                    vld = P.dma("sync", w2b,
                                d_w2t[fbk * 128:(fbk + 1) * 128,
                                      g * 512:(g + 1) * 512])
                    P.wait("tensor", vld)
                    if fbk == 0:
                        P.wait("tensor", "act", v_h_done)
                        P.wait("tensor", f2_psum_free[0], f2_psum_free[1])
                        if th == 0 and g == 0:
                            P.wait("tensor", "act", v_woev_last)
                    for sl in range(4):
                        fn = (lambda e, w2b=w2b, sl=sl, fbk=fbk, g=g: e.matmul(
                            f2slots[sl],
                            w2b[:, sl * 128:(sl + 1) * 128],
                            hbuf[:, H_H + fbk * 512:H_H + (fbk + 1) * 512],
                            start=(fbk == 0), stop=(fbk == 63)))
                        if sl == 3:
                            v_f2[fbk] = P.op("tensor", fn, inc="mm")
                        else:
                            P.op("tensor", fn)
                    w2_free[p] = ("mm", v_f2[fbk])
                v_last_evac = 0
                for sl in range(4):
                    jd = g * 4 + sl
                    ppar = jd % 2
                    P.wait("scalar", "mm", v_f2[63])
                    P.wait("scalar", fa_slot_free[sl][0], fa_slot_free[sl][1])
                    v_silu = P.op("scalar", lambda e, sl=sl, jd=jd: e.activation(
                        out=fa[:, sl * 512:(sl + 1) * 512], in_=f2slots[sl],
                        func=AF.Silu, bias=s_b2[:, jd:jd + 1], scale=1.0),
                        inc="act")
                    v_last_evac = v_silu
                    P.wait("vector", "act", v_silu)
                    if th == 0 and g == 0:
                        P.wait("vector", "act", v_woev_last)
                    P.wait("vector", pbf_free[ppar][0], pbf_free[ppar][1])
                    v_p = P.op("vector", lambda e, sl=sl, jd=jd, t0=t0, ppar=ppar:
                               e.tensor_tensor(
                                   out=hbuf[:, H_PBF + ppar * 512:
                                            H_PBF + (ppar + 1) * 512],
                                   in0=fa[:, sl * 512:(sl + 1) * 512],
                                   in1=qbuf[:, jd * T + t0: jd * T + t0 + 512],
                                   op=ALU.mult), inc="vec")
                    fa_slot_free[sl] = ("vec", v_p)
                    P.wait("tensor", "vec", v_p)
                    P.wait("tensor", pf2_free[ppar][0], pf2_free[ppar][1])
                    v_tp5 = 0
                    for s4 in range(4):
                        fn = (lambda e, s4=s4, ppar=ppar: e.transpose(
                            P_f[:, ppar * 1024 + s4 * 128:
                                ppar * 1024 + (s4 + 1) * 128],
                            hbuf[:, H_PBF + ppar * 512 + s4 * 128:
                                 H_PBF + ppar * 512 + (s4 + 1) * 128],
                            ident[:, :]))
                        if s4 == 3:
                            v_tp5 = P.op("tensor", fn, inc="mm")
                        else:
                            P.op("tensor", fn)
                    pbf_free[ppar] = ("mm", v_tp5)
                    P.wait("sync", fb_slot_free[sl][0], fb_slot_free[sl][1])
                    vxl = P.dma("sync",
                                chunked(fb[:, sl * 512:(sl + 1) * 512], 128),
                                x_d[th * 512:(th + 1) * 512,
                                    jd * 128:(jd + 1) * 128].rearrange(
                                        "(s p) d -> p s d", p=128))
                    P.wait("vector", "mm", v_tp5)
                    P.wait("vector", vxl)
                    P.wait("vector", fout_free[ppar])
                    v_ot = P.op("vector", lambda e, sl=sl, ppar=ppar:
                                e.tensor_tensor(
                                    out=fout[:, ppar * 512:(ppar + 1) * 512],
                                    in0=P_f[:, ppar * 1024:ppar * 1024 + 512],
                                    in1=fb[:, sl * 512:(sl + 1) * 512],
                                    op=ALU.add), inc="vec")
                    pf2_free[ppar] = ("vec", v_ot)
                    fb_slot_free[sl] = ("vec", v_ot)
                    P.wait("sync", "vec", v_ot)
                    v_out_dma = P.dma(
                        "sync",
                        out_d[th * 512:(th + 1) * 512,
                              jd * 128:(jd + 1) * 128].rearrange(
                                  "(s p) d -> p s d", p=128),
                        chunked(fout[:, ppar * 512:(ppar + 1) * 512], 128))
                    fout_free[ppar] = v_out_dma
                f2_psum_free = ("act", v_last_evac)

        if DEBUG:
            P.wait("sync", "vec", C["vec"])
            P.wait("sync", "act", C["act"])
            P.dma("sync", o_xn[:, :], xnbuf[:, :])
            P.dma("sync", o_st[:, :], qbuf[:, :])
            P.dma("sync", o_sum[:, :], ssmall[0:1, 2:4])
            P.dma("sync", o_kg[:, :], d_k[:, :])
            P.dma("sync", o_vg[:, :], d_v[:, :])
            for r in range(16):
                P.dma("sync", o_w1t[r * 128:(r + 1) * 128, :],
                      d_w1t[r * 128:(r + 1) * 128, :])
            for r in range(64):
                P.dma("sync", o_w2t[r * 128:(r + 1) * 128, :],
                      d_w2t[r * 128:(r + 1) * 128, :])

        for s in SQ:
            P.wait("sync", s, C[s])

        with nc.Block() as block:
            @block.sync
            def _(e):
                for fn in P.items["sync"]:
                    fn(e)

            @block.gpsimd
            def _(e):
                for fn in P.items["gpsimd"]:
                    fn(e)

            @block.tensor
            def _(e):
                for fn in P.items["tensor"]:
                    fn(e)

            @block.vector
            def _(e):
                for fn in P.items["vector"]:
                    fn(e)

            @block.scalar
            def _(e):
                for fn in P.items["scalar"]:
                    fn(e)

    return nc


_NC_CACHE = None


def kernel(L0_attn_x, L0_attn_Wq, L0_attn_bq, L0_attn_Wk, L0_attn_bk,
           L0_attn_Wv, L0_attn_bv, L0_attn_Wo, L0_attn_bo,
           L0_ffn_w1, L0_ffn_b1, L0_ffn_w2, L0_ffn_b2):
    global _NC_CACHE
    x = np.ascontiguousarray(np.asarray(L0_attn_x, dtype=np.float32))
    wq = np.ascontiguousarray(np.asarray(L0_attn_Wq, dtype=np.float32))
    wk = np.ascontiguousarray(np.asarray(L0_attn_Wk, dtype=np.float32))
    wv = np.ascontiguousarray(np.asarray(L0_attn_Wv, dtype=np.float32))
    wo = np.ascontiguousarray(np.asarray(L0_attn_Wo, dtype=np.float32))
    w1 = np.ascontiguousarray(np.asarray(L0_ffn_w1, dtype=np.float32))
    w2 = np.ascontiguousarray(np.asarray(L0_ffn_w2, dtype=np.float32))

    def fold(b, n):
        return np.ascontiguousarray(
            np.asarray(b, dtype=np.float32).reshape(n, 128).T)

    bq2 = fold(L0_attn_bq, 16)
    bk2 = fold(L0_attn_bk, 16)
    bo2 = fold(L0_attn_bo, 16)
    b12 = fold(L0_ffn_b1, 64)
    b22 = fold(L0_ffn_b2, 16)
    bvr = np.ascontiguousarray(
        np.asarray(L0_attn_bv, dtype=np.float32).reshape(1, D))

    if _NC_CACHE is None:
        _NC_CACHE = build_program()
    nc = _NC_CACHE

    in_maps = []
    for c in range(8):
        b, sl = c // 4, c % 4
        in_maps.append({
            "x": np.ascontiguousarray(x[b, sl * T:(sl + 1) * T, :]),
            "wq": wq, "wk": wk, "wv": wv, "wo": wo,
            "bq": bq2, "bk": bk2, "bo": bo2, "bv_row": bvr,
            "b1": b12, "b2": b22,
            "w1s": np.ascontiguousarray(w1[:, c * 256:(c + 1) * 256]),
            "w2s": np.ascontiguousarray(w2[:, c * 1024:(c + 1) * 1024]),
        })
    res = run_bass_kernel_spmd(nc, in_maps, core_ids=list(range(8)))
    out = np.empty((2, 4096, D), dtype=np.float32)
    for c in range(8):
        b, sl = c // 4, c % 4
        out[b, sl * T:(sl + 1) * T, :] = res.results[c]["out"]
    return out


# version 03
# speedup vs baseline: 1.3915x; 1.3915x over previous
import numpy as np
import ml_dtypes
import concourse.bass as bass
import concourse.mybir as mybir
from concourse.bass_utils import run_bass_kernel_spmd

F32 = mybir.dt.float32
BF16 = mybir.dt.bfloat16
AF = mybir.ActivationFunctionType
ALU = mybir.AluOpType
AX = mybir.AxisListType
bf = ml_dtypes.bfloat16

D = 2048
DFF = 8192
T = 1024
NCH = 16
PAIRS = 16
SCALE = float(1.0 / np.sqrt(32.0))
BIG = float(2.0 ** 63)
HALFBIG = float(0.5 * 2.0 ** 63)
NMAT = float(DFF * D)
GROUPS_KV = [[0, 1, 2, 3], [4, 5, 6, 7]]
GROUP_ALL = [[0, 1, 2, 3, 4, 5, 6, 7]]

DEBUG = False

SEM = {}
NSQ = 8
NGQ = 4
SQ = tuple(f"sq{i}" for i in range(NSQ))
GQ = tuple(f"gq{i}" for i in range(NGQ))
SEMNAMES = SQ + GQ + ("mm", "act", "vec", "gp",
                      "ccr", "cck", "ccv", "cw1", "cw2")


class Plan:

    ENGS = ("sync", "gpsimd", "tensor", "vector", "scalar")

    def __init__(self):
        self.items = {e: [] for e in self.ENGS}
        self.ctr = {s: 0 for s in SEMNAMES}
        self.rr = {"sync": 0, "gpsimd": 0}

    def add(self, eng, fn):
        self.items[eng].append(fn)

    def wait(self, eng, sem, val=None):
        if val is None:
            tok = sem
            if tok is None:
                return
            if isinstance(tok, list):
                best = {}
                for s, v in tok:
                    if s is not None and v > best.get(s, 0):
                        best[s] = v
                for s, v in best.items():
                    self.add(eng, lambda e, s=s, v=v: e.wait_ge(SEM[s], v))
                return
            sem, val = tok
        if sem is None or val is None or val <= 0:
            return
        self.add(eng, lambda e, s=sem, v=val: e.wait_ge(SEM[s], v))

    def dma(self, eng, out_ap, in_ap, **kw):
        pool = SQ if eng == "sync" else GQ
        s = pool[self.rr[eng] % len(pool)]
        self.rr[eng] += 1
        self.ctr[s] += 16
        v = self.ctr[s]
        self.add(eng, lambda e, o=out_ap, i=in_ap, kw=dict(kw), s=s: e.dma_start(
            out=o, in_=i, **kw).then_inc(SEM[s], 16))
        return (s, v)

    def op(self, eng, fn, inc=None):
        if inc is None:
            self.add(eng, fn)
            return None
        self.ctr[inc] += 1
        v = self.ctr[inc]
        self.add(eng, lambda e, f=fn, s=inc: f(e).then_inc(SEM[s], 1))
        return v

    def collective(self, sem, kind, op_, groups, in_ap, out_ap):
        self.ctr[sem] += 1
        v = self.ctr[sem]
        self.add("gpsimd", lambda e, k=kind, o=op_, g=groups, i=in_ap, oo=out_ap,
                 s=sem: e.collective_compute(
                     k, o, replica_groups=g, ins=[i.opt()],
                     outs=[oo.opt()]).then_inc(SEM[s], 1))
        return v


def build_program():
    nc = bass.Bass("TRN2", target_bir_lowering=False)

    x_d = nc.dram_tensor("x", [T, D], F32, kind="ExternalInput").ap()
    wq_d = nc.dram_tensor("wq", [D, D], F32, kind="ExternalInput").ap()
    wk_d = nc.dram_tensor("wk", [D, D], F32, kind="ExternalInput").ap()
    wv_d = nc.dram_tensor("wv", [D, D], F32, kind="ExternalInput").ap()
    wo_d = nc.dram_tensor("wo", [D, D], F32, kind="ExternalInput").ap()
    bq_d = nc.dram_tensor("bq", [128, 16], F32, kind="ExternalInput").ap()
    bk_d = nc.dram_tensor("bk", [128, 16], F32, kind="ExternalInput").ap()
    bo_d = nc.dram_tensor("bo", [128, 16], F32, kind="ExternalInput").ap()
    bv_row_d = nc.dram_tensor("bv_row", [1, D], F32, kind="ExternalInput").ap()
    b1_d = nc.dram_tensor("b1", [128, 64], F32, kind="ExternalInput").ap()
    b2_d = nc.dram_tensor("b2", [128, 16], F32, kind="ExternalInput").ap()
    w1s_d = nc.dram_tensor("w1s", [DFF, 256], F32, kind="ExternalInput").ap()
    w2s_d = nc.dram_tensor("w2s", [D, 1024], F32, kind="ExternalInput").ap()
    out_d = nc.dram_tensor("out", [T, D], F32, kind="ExternalOutput").ap()
    if DEBUG:
        o_xn = nc.dram_tensor("o_xn", [128, NCH * T], BF16,
                              kind="ExternalOutput").ap()
        o_st = nc.dram_tensor("o_st", [128, NCH * T], BF16,
                              kind="ExternalOutput").ap()
        o_fe = nc.dram_tensor("o_fe", [128, NCH * T], BF16,
                              kind="ExternalOutput").ap()
        o_h = nc.dram_tensor("o_h", [128, 32768], BF16,
                             kind="ExternalOutput").ap()
        o_kg = nc.dram_tensor("o_kg", [4 * D, T], BF16,
                              kind="ExternalOutput").ap()
        o_vg = nc.dram_tensor("o_vg", [4 * T, 2080], BF16,
                              kind="ExternalOutput").ap()
        o_w1t = nc.dram_tensor("o_w1t", [D, DFF], BF16,
                               kind="ExternalOutput").ap()
        o_w2t = nc.dram_tensor("o_w2t", [DFF, D], BF16,
                               kind="ExternalOutput").ap()
        o_sum = nc.dram_tensor("o_sum", [1, 2], F32, kind="ExternalOutput").ap()
        o_q = nc.dram_tensor("o_q", [128, NCH * T], BF16,
                             kind="ExternalOutput").ap()

    d_k_in = nc.dram_tensor("d_k_in", [D, T], BF16).ap()
    d_k = nc.dram_tensor("d_k", [4 * D, T], BF16).ap()
    d_v_in = nc.dram_tensor("d_v_in", [T, 2080], BF16).ap()
    d_v = nc.dram_tensor("d_v", [4 * T, 2080], BF16).ap()
    d_w1t_in = nc.dram_tensor("d_w1t_in", [256, DFF], BF16).ap()
    d_w1t = nc.dram_tensor("d_w1t", [D, DFF], BF16).ap()
    d_w2t_in = nc.dram_tensor("d_w2t_in", [1024, D], BF16).ap()
    d_w2t = nc.dram_tensor("d_w2t", [DFF, D], BF16).ap()
    d_sum_in = nc.dram_tensor("d_sum_in", [1, 2], F32).ap()
    d_sum = nc.dram_tensor("d_sum", [1, 2], F32).ap()

    from contextlib import ExitStack
    ctx = ExitStack()
    with ctx:
        def sbt(name, shape, dt):
            return ctx.enter_context(nc.sbuf_tensor(name, shape, dt))

        def pst(name, shape, dt):
            return ctx.enter_context(nc.psum_tensor(name, shape, dt))

        xnbuf = sbt("xnbuf", [128, NCH * T], BF16)
        qbuf = sbt("qbuf", [128, NCH * T], BF16)
        hbuf = sbt("hbuf", [128, 37120], BF16)
        wbufA = sbt("wbufA", [128, 8192], BF16)
        wbufB = sbt("wbufB", [128, 8192], BF16)
        fa = sbt("fa", [128, 2048], F32)
        fb = sbt("fb", [128, 2048], F32)
        fout = sbt("fout", [128, 1024], F32)
        s_bq = sbt("s_bq", [128, 16], F32)
        s_bk = sbt("s_bk", [128, 16], F32)
        s_bo = sbt("s_bo", [128, 16], F32)
        s_b1 = sbt("s_b1", [128, 64], F32)
        s_b2 = sbt("s_b2", [128, 16], F32)
        s_bvr = sbt("s_bvr", [1, 2048], F32)
        ident = sbt("ident", [128, 128], BF16)
        ones1 = sbt("ones1", [1, 128], F32)
        ones128 = sbt("ones128", [128, 1], F32)
        ones64f = sbt("ones64f", [1, 64], F32)
        stat = sbt("stat", [128, 8], F32)
        epst = sbt("epst", [128, 1], F32)
        acc = sbt("acc", [128, 2], F32)
        acct = sbt("acct", [128, 1], F32)
        ssmall = sbt("ssmall", [1, 8], F32)
        inv_bc = sbt("inv_bc", [128, 4], F32)
        recA = sbt("recA", [1, 512], F32)
        recB = sbt("recB", [1, 512], F32)
        bcbufA = sbt("bcbufA", [64, 512], F32)
        bcbufB = sbt("bcbufB", [64, 512], F32)

        H_XN = 0
        H_KT = 0
        H_VST = 2048
        H_TU, H_TV, H_TT = 0, 2048, 4096
        H_WST = 6144
        H_KB = 0
        H_VB = 8192
        H_EA = 16512
        H_EB = 18560
        H_FE = 20608
        H_H = 0
        H_PBF = 32768

        P_a = pst("P_a", [128, 1024], F32)
        P_b = pst("P_b", [128, 1024], F32)
        P_c = pst("P_c", [128, 512], F32)
        P_d = pst("P_d", [128, 512], F32)
        P_f = pst("P_f", [128, 2048], BF16)

        SEM.clear()
        for s in SEMNAMES:
            SEM[s] = ctx.enter_context(nc.semaphore(s))

        P = Plan()
        C = P.ctr

        P.op("gpsimd", lambda e: e.memset(ident[:, :], 0.0))
        v_id = P.op("gpsimd", lambda e: e.affine_select(
            out=ident[:, :], in_=ident[:, :], compare_op=ALU.not_equal,
            fill=1.0, base=0, pattern=[[-1, 128]], channel_multiplier=1),
            inc="gp")
        P.op("gpsimd", lambda e: e.memset(ones1[:, :], 1.0))
        P.op("gpsimd", lambda e: e.memset(ones128[:, :], 1.0))
        P.op("gpsimd", lambda e: e.memset(ones64f[:, :], 1.0))
        v_ones = P.op("gpsimd", lambda e: e.memset(epst[:, :], 1e-5), inc="gp")

        toks_bias = [P.dma("sync", s_bq[:, :], bq_d[:, :]),
                     P.dma("sync", s_bk[:, :], bk_d[:, :]),
                     P.dma("sync", s_bo[:, :], bo_d[:, :]),
                     P.dma("sync", s_b1[:, :], b1_d[:, :]),
                     P.dma("sync", s_b2[:, :], b2_d[:, :])]
        v_bias = P.dma("sync", s_bvr[:, :], bv_row_d[:, :])

        v_acc0 = P.op("vector", lambda e: e.memset(acc[:, :], 0.0), inc="vec")
        fa_free = ("vec", 0)
        for which, src_ap, ntile, tw in ((0, w1s_d, 64, 256), (1, w2s_d, 16, 1024)):
            for k in range(ntile):
                P.wait("sync", fa_free[0], fa_free[1])
                vl = P.dma("sync", fa[:, 0:tw], src_ap[k * 128:(k + 1) * 128, :])
                P.wait("vector", vl)
                P.op("vector", lambda e, tw=tw: e.reduce_sum(
                    acct[:, 0:1], fa[:, 0:tw], axis=AX.X,
                    apply_absolute_value=True), inc="vec")
                fa_free = ("vec", C["vec"])
                P.wait("vector", "vec", C["vec"])
                P.op("vector", lambda e, w=which: e.tensor_add(
                    acc[:, w:w + 1], acc[:, w:w + 1], acct[:, 0:1]), inc="vec")
        v_abs_done = C["vec"]
        P.wait("tensor", "vec", v_abs_done)
        P.wait("tensor", "gp", v_ones)
        v_sum_mm = P.op("tensor", lambda e: e.matmul(
            P_c[0:1, 0:2], ones128[:, 0:1], acc[:, :], start=True, stop=True),
            inc="mm")
        P.wait("vector", "mm", v_sum_mm)
        v_sum_cp = P.op("vector", lambda e: e.tensor_copy(
            ssmall[0:1, 0:2], P_c[0:1, 0:2]), inc="vec")
        P.wait("sync", "vec", v_sum_cp)
        v_sum_st = P.dma("sync", d_sum_in[:, :], ssmall[0:1, 0:2])
        P.wait("gpsimd", v_sum_st)
        v_ar = P.collective("ccr", "AllReduce", ALU.add, GROUP_ALL, d_sum_in, d_sum)

        xnbf = hbuf[:, H_XN:H_XN + 2048]
        xnbf_free = ("mm", 0)
        pf_evac = {0: 0, 1: 0}
        v_tph = 0
        for i in range(8):
            P.wait("sync", fa_free[0], fa_free[1])
            vl = P.dma("sync", fa[:, :], x_d[i * 128:(i + 1) * 128, :])
            P.wait("vector", vl)
            P.op("vector", lambda e: e.reduce_sum(
                stat[:, 0:1], fa[:, :], axis=AX.X, negate=True), inc="vec")
            P.wait("vector", "vec", C["vec"])
            v_nm = P.op("vector", lambda e: e.tensor_scalar_mul(
                stat[:, 1:2], stat[:, 0:1], 1.0 / 2048.0), inc="vec")
            P.wait("scalar", "vec", v_nm)
            P.op("scalar", lambda e: e.activation(
                out=fb[:, :], in_=fa[:, :], func=AF.Identity,
                bias=stat[:, 1:2], scale=1.0), inc="act")
            P.wait("scalar", "act", C["act"])
            v_sqr = P.op("scalar", lambda e: e.activation(
                out=fa[:, :], in_=fb[:, :], func=AF.Square,
                accum_out=stat[:, 2:3]), inc="act")
            fa_free = ("act", v_sqr)
            P.wait("scalar", "act", C["act"])
            if i == 0:
                P.wait("scalar", "gp", v_ones)
            v_std = P.op("scalar", lambda e: e.activation(
                out=stat[:, 3:4], in_=stat[:, 2:3], func=AF.Sqrt,
                bias=epst[:, 0:1], scale=1.0 / 2048.0), inc="act")
            P.wait("vector", "act", v_std)
            v_rstd = P.op("vector", lambda e: e.reciprocal(
                stat[:, 4:5], stat[:, 3:4]), inc="vec")
            P.wait("vector", "vec", C["vec"])
            v_rstd2 = P.op("vector", lambda e: e.tensor_copy(
                stat[:, 5:6], stat[:, 4:5]), inc="vec")
            P.wait("scalar", "vec", v_rstd2)
            P.wait("scalar", xnbf_free[0], xnbf_free[1])
            v_xn = P.op("scalar", lambda e: e.activation(
                out=xnbf[:, :], in_=fb[:, :], func=AF.Identity,
                scale=stat[:, 5:6]), inc="act")
            P.wait("tensor", "act", v_xn)
            if i == 0:
                P.wait("tensor", "gp", v_id)
            for half in range(2):
                P.wait("tensor", "vec", pf_evac[half])
                for k in range(8):
                    kk = half * 8 + k
                    fn = lambda e, kk=kk, k=k, half=half: e.transpose(
                        P_f[:, half * 1024 + k * 128: half * 1024 + k * 128 + 128],
                        xnbf[:, kk * 128:kk * 128 + 128], ident[:, :])
                    if k == 7:
                        v_tph = P.op("tensor", fn, inc="mm")
                    else:
                        P.op("tensor", fn)
                P.wait("vector", "mm", v_tph)
                out3 = xnbuf[:, half * 8 * T: (half * 8 + 8) * T].rearrange(
                    "p (c t) -> p c t", t=T)[:, :, i * 128:i * 128 + 128]
                in3 = P_f[:, half * 1024: half * 1024 + 1024].rearrange(
                    "p (c t) -> p c t", t=128)
                pf_evac[half] = P.op(
                    "vector", lambda e, o=out3, ii=in3: e.tensor_copy(o, ii),
                    inc="vec")
            xnbf_free = ("mm", v_tph)
        v_xnT_done = max(pf_evac.values())

        def band_ap(w_ap, jd):
            return w_ap[:, jd * 128:(jd + 1) * 128].rearrange(
                "(i p) d -> p i d", p=128)

        def chunked(sb_ap, inner):
            return sb_ap.rearrange("p (i d) -> p i d", d=inner)

        band_free = {0: ("gp", 0), 1: ("gp", 0)}
        psum_free = {0: ("act", 0), 1: ("act", 0)}
        kst_free = {0: None, 1: None}
        toks_kst = []
        v_kmm_last = 0
        for jd in range(16):
            p = jd % 2
            wb = (wbufA if p == 0 else wbufB)[:, 0:2048]
            P.wait("gpsimd", band_free[p][0], band_free[p][1])
            vw = P.dma("gpsimd", chunked(wb, 128), band_ap(wk_d, jd))
            P.wait("tensor", vw)
            P.wait("tensor", psum_free[p][0], psum_free[p][1])
            if jd == 0:
                P.wait("tensor", "vec", v_xnT_done)
            P_x = P_a if p == 0 else P_b
            for hh in range(2):
                for i in range(16):
                    fn = lambda e, wb=wb, i=i, hh=hh, P_x=P_x: e.matmul(
                        P_x[:, hh * 512:(hh + 1) * 512],
                        wb[:, i * 128:(i + 1) * 128],
                        xnbuf[:, i * T + hh * 512: i * T + (hh + 1) * 512],
                        start=(i == 0), stop=(i == 15))
                    if hh == 1 and i == 15:
                        v_kmm = P.op("tensor", fn, inc="mm")
                    else:
                        P.op("tensor", fn)
            band_free[p] = ("mm", v_kmm)
            v_kmm_last = v_kmm
            P.wait("scalar", "mm", v_kmm)
            P.wait("scalar", kst_free[p])
            if jd == 0:
                P.wait("scalar", toks_bias)
            if jd < 2:
                P.wait("scalar", "mm", v_tph)
            kts = hbuf[:, H_KT + p * 1024: H_KT + (p + 1) * 1024]
            v_kev = P.op("scalar", lambda e, P_x=P_x, jd=jd, kts=kts: e.activation(
                out=kts, in_=P_x[:, :], func=AF.Identity,
                bias=s_bk[:, jd:jd + 1], scale=1.0), inc="act")
            psum_free[p] = ("act", v_kev)
            P.wait("sync", "act", v_kev)
            v_kst = P.dma("sync", d_k_in[jd * 128:(jd + 1) * 128, :], kts)
            toks_kst.append(v_kst)
            kst_free[p] = v_kst
        P.wait("gpsimd", toks_kst)
        v_agk = P.collective("cck", "AllGather", ALU.bypass, GROUPS_KV, d_k_in, d_k)

        P.wait("tensor", v_bias)
        P.wait("tensor", "gp", v_ones)
        P.wait("tensor", "vec", v_sum_cp)
        v_bvbc = 0
        for q in range(4):
            vq = P.op("tensor", lambda e, q=q: e.matmul(
                P_c[:, :], ones1[0:1, :], s_bvr[0:1, q * 512:(q + 1) * 512],
                start=True, stop=True), inc="mm")
            P.wait("vector", "mm", vq)
            v_bvbc = P.op("vector", lambda e, q=q: e.tensor_copy(
                fa[:, q * 512:(q + 1) * 512], P_c[:, :]), inc="vec")
            P.wait("tensor", "vec", v_bvbc)

        P.wait("gpsimd", "vec", v_xnT_done)
        v_vst_ms = 0
        for tt in range(8):
            v_vst_ms = P.op("gpsimd", lambda e, tt=tt: e.memset(
                hbuf[:, H_VST + tt * 2080: H_VST + (tt + 1) * 2080], 1.0),
                inc="gp")
        v_vev = {}
        vband_free = {0: band_free[0], 1: band_free[1]}
        vpsum_free = {0: ("vec", v_bvbc), 1: ("vec", v_bvbc)}
        for q in range(4):
            p = q % 2
            wb = wbufA if p == 0 else wbufB
            P.wait("gpsimd", vband_free[p][0], vband_free[p][1])
            vw = P.dma("gpsimd", chunked(wb[:, :], 512),
                       wv_d[:, q * 512:(q + 1) * 512].rearrange(
                           "(i p) d -> p i d", p=128))
            v_vmm = 0
            for tt in range(8):
                pp = tt % 2
                P_x = P_c if pp == 0 else P_d
                P.wait("tensor", vw)
                P.wait("tensor", vpsum_free[pp][0], vpsum_free[pp][1])
                for i in range(16):
                    fn = lambda e, i=i, wb=wb, tt=tt, P_x=P_x: e.matmul(
                        P_x[:, :],
                        xnbuf[:, i * T + tt * 128: i * T + tt * 128 + 128],
                        wb[:, i * 512:(i + 1) * 512],
                        start=(i == 0), stop=(i == 15))
                    if i == 15:
                        v_vmm = P.op("tensor", fn, inc="mm")
                    else:
                        P.op("tensor", fn)
                P.wait("vector", "mm", v_vmm)
                if q == 0 and tt == 0:
                    P.wait("vector", "gp", v_vst_ms)
                base = H_VST + tt * 2080 + q * 520
                o3 = hbuf[:, base: base + 520].rearrange(
                    "p (h e) -> p h e", e=65)[:, :, 0:64]
                i3 = P_x[:, :].rearrange("p (h e) -> p h e", e=64)
                b3 = fa[:, q * 512:(q + 1) * 512].rearrange(
                    "p (h e) -> p h e", e=64)
                vv = P.op("vector", lambda e, o3=o3, i3=i3, b3=b3: e.tensor_tensor(
                    out=o3, in0=i3, in1=b3, op=ALU.add), inc="vec")
                v_vev[(tt, q)] = vv
                vpsum_free[pp] = ("vec", vv)
            vband_free[p] = ("mm", v_vmm)
        toks_vst = []
        for tt in range(8):
            P.wait("sync", "vec", v_vev[(tt, 3)])
            toks_vst.append(P.dma(
                "sync", d_v_in[tt * 128:(tt + 1) * 128, :],
                hbuf[:, H_VST + tt * 2080: H_VST + (tt + 1) * 2080]))
        P.wait("gpsimd", toks_vst)
        v_agv = P.collective("ccv", "AllGather", ALU.bypass, GROUPS_KV, d_v_in, d_v)

        qband_free = {0: vband_free[0], 1: vband_free[1]}
        qpsum_free = {0: psum_free[0], 1: psum_free[1]}
        v_qev = 0
        for jd in range(16):
            p = jd % 2
            wb = (wbufA if p == 0 else wbufB)[:, 0:2048]
            P.wait("gpsimd", qband_free[p][0], qband_free[p][1])
            vw = P.dma("gpsimd", chunked(wb, 128), band_ap(wq_d, jd))
            P.wait("tensor", vw)
            P.wait("tensor", qpsum_free[p][0], qpsum_free[p][1])
            P_x = P_a if p == 0 else P_b
            for hh in range(2):
                for i in range(16):
                    fn = lambda e, wb=wb, i=i, hh=hh, P_x=P_x: e.matmul(
                        P_x[:, hh * 512:(hh + 1) * 512],
                        wb[:, i * 128:(i + 1) * 128],
                        xnbuf[:, i * T + hh * 512: i * T + (hh + 1) * 512],
                        start=(i == 0), stop=(i == 15))
                    if hh == 1 and i == 15:
                        v_qmm = P.op("tensor", fn, inc="mm")
                    else:
                        P.op("tensor", fn)
            qband_free[p] = ("mm", v_qmm)
            P.wait("scalar", "mm", v_qmm)
            v_qev = P.op("scalar", lambda e, P_x=P_x, jd=jd: e.activation(
                out=qbuf[:, jd * T:(jd + 1) * T], in_=P_x[:, :],
                func=AF.Identity, bias=s_bq[:, jd:jd + 1], scale=1.0), inc="act")
            qpsum_free[p] = ("act", v_qev)

        P.wait("sync", "ccr", v_ar)
        v_sumld = P.dma("sync", ssmall[0:1, 2:4], d_sum[:, :])
        P.wait("vector", v_sumld)
        P.op("vector", lambda e: e.tensor_scalar(
            out=ssmall[0:1, 4:6], in0=ssmall[0:1, 2:4], scalar1=1.0 / NMAT,
            scalar2=1e-8, op0=ALU.mult, op1=ALU.max), inc="vec")
        P.wait("vector", "vec", C["vec"])
        v_inv = P.op("vector", lambda e: e.reciprocal(
            ssmall[0:1, 6:8], ssmall[0:1, 4:6]), inc="vec")
        P.wait("vector", "vec", C["vec"])
        P.wait("tensor", "vec", v_inv)
        P.wait("tensor", vpsum_free[1][0], vpsum_free[1][1])
        v_invb = P.op("tensor", lambda e: e.matmul(
            P_d[:, 0:2], ones1[0:1, :], ssmall[0:1, 6:8],
            start=True, stop=True), inc="mm")
        P.wait("vector", "mm", v_invb)
        P.op("vector", lambda e: e.tensor_scalar_mul(
            inv_bc[:, 0:2], P_d[:, 0:2], BIG), inc="vec")
        v_invbc = P.op("vector", lambda e: e.tensor_scalar_mul(
            inv_bc[:, 2:4], P_d[:, 0:2], -BIG), inc="vec")

        v_wst = {}
        pf_free = {0: ("vec", pf_evac[0]), 1: ("vec", pf_evac[1])}
        tern_free = ("mm", 0)
        v_w1std = v_w2std = 0
        for which, src_ap, ntile, tw in ((0, w1s_d, 64, 256), (1, w2s_d, 16, 1024)):
            ntp = tw // 128
            v_tp = 0
            for k in range(ntile):
                P.wait("sync", fa_free[0], fa_free[1])
                if which == 0 and k == 0:
                    P.wait("sync", "vec", v_vev[(7, 3)])
                vl = P.dma("sync", fa[:, 0:tw], src_ap[k * 128:(k + 1) * 128, :])
                P.wait("vector", vl)
                P.wait("vector", "vec", v_invbc)
                if which == 0 and k == 0:
                    P.wait("vector", toks_vst)
                    P.wait("vector", toks_kst)
                P.wait("vector", tern_free[0], tern_free[1])
                m = which
                P.op("vector", lambda e, tw=tw, m=m: e.tensor_scalar(
                    out=fb[:, 0:tw], in0=fa[:, 0:tw], scalar1=inv_bc[:, m:m + 1],
                    scalar2=-HALFBIG, op0=ALU.mult, op1=ALU.add), inc="vec")
                P.op("vector", lambda e, tw=tw: e.tensor_scalar(
                    out=hbuf[:, H_TU:H_TU + tw], in0=fb[:, 0:tw], scalar1=0.0,
                    scalar2=1.0, op0=ALU.max, op1=ALU.min), inc="vec")
                v_u2 = P.op("vector", lambda e, tw=tw, m=m: e.tensor_scalar(
                    out=fb[:, 0:tw], in0=fa[:, 0:tw],
                    scalar1=inv_bc[:, 2 + m:3 + m],
                    scalar2=-HALFBIG, op0=ALU.mult, op1=ALU.add), inc="vec")
                fa_free = ("vec", v_u2)
                P.op("vector", lambda e, tw=tw: e.tensor_scalar(
                    out=hbuf[:, H_TV:H_TV + tw], in0=fb[:, 0:tw], scalar1=0.0,
                    scalar2=1.0, op0=ALU.max, op1=ALU.min), inc="vec")
                v_tt = P.op("vector", lambda e, tw=tw: e.tensor_tensor(
                    out=hbuf[:, H_TT:H_TT + tw], in0=hbuf[:, H_TU:H_TU + tw],
                    in1=hbuf[:, H_TV:H_TV + tw], op=ALU.subtract), inc="vec")
                P.wait("tensor", "vec", v_tt)
                half = k % 2
                P.wait("tensor", pf_free[half][0], pf_free[half][1])
                for j in range(ntp):
                    fn = lambda e, j=j, half=half: e.transpose(
                        P_f[:, half * 1024 + j * 128: half * 1024 + (j + 1) * 128],
                        hbuf[:, H_TT + j * 128:H_TT + (j + 1) * 128], ident[:, :])
                    if j == ntp - 1:
                        v_tp = P.op("tensor", fn, inc="mm")
                    else:
                        P.op("tensor", fn)
                tern_free = ("mm", v_tp)
                P.wait("vector", "mm", v_tp)
                if which == 1 and k == 0:
                    P.wait("vector", toks_w1st)
                cw = 8192 if which == 0 else 2048
                o3 = hbuf[:, H_WST:H_WST + ntp * cw].rearrange(
                    "p (j c) -> p j c", c=cw)[:, :, k * 128:k * 128 + 128]
                i3 = P_f[:, half * 1024: half * 1024 + ntp * 128].rearrange(
                    "p (j c) -> p j c", c=128)
                pf_free[half] = ("vec", P.op(
                    "vector", lambda e, o3=o3, i3=i3: e.tensor_copy(o3, i3),
                    inc="vec"))
            nrow = ntp
            cw = 8192 if which == 0 else 2048
            dst = d_w1t_in if which == 0 else d_w2t_in
            toks_std = []
            for j in range(nrow):
                P.wait("sync", "vec", pf_free[(ntile - 1) % 2][1])
                P.wait("sync", "vec", pf_free[(ntile - 2) % 2][1])
                toks_std.append(P.dma(
                    "sync", dst[j * 128:(j + 1) * 128, :],
                    hbuf[:, H_WST + j * cw:H_WST + (j + 1) * cw]))
            if which == 0:
                toks_w1st = toks_std
                P.wait("gpsimd", toks_w1st)
                v_agw1 = P.collective("cw1", "AllGather", ALU.bypass, GROUP_ALL,
                                      d_w1t_in, d_w1t)
            else:
                toks_w2st = toks_std
                P.wait("gpsimd", toks_w2st)
                v_agw2 = P.collective("cw2", "AllGather", ALU.bypass, GROUP_ALL,
                                      d_w2t_in, d_w2t)

        kvld = {}
        feats_evac = {}
        v_feats_evac_last = 0
        exp_free = ("mm", 0)

        for j in range(PAIRS):
            p = j % 2
            kb = hbuf[:, H_KB + p * 4096: H_KB + (p + 1) * 4096]
            vb = hbuf[:, H_VB + p * 4160: H_VB + (p + 1) * 4160]
            if j >= 2:
                P.wait("sync", "vec", feats_evac[(j - 2, 1)])
            else:
                P.wait("sync", "cck", v_agk)
                P.wait("sync", "ccv", v_agv)
                P.wait("sync", toks_w2st)
                P.wait("sync", "mm", tern_free[1])
            toks_kv = []
            for r in range(4):
                toks_kv.append(P.dma(
                    "sync", kb[:, r * 1024:(r + 1) * 1024],
                    d_k[r * 2048 + j * 128: r * 2048 + (j + 1) * 128, :]))
            for r in range(4):
                toks_kv.append(P.dma(
                    "sync", chunked(vb[:, r * 1040:(r + 1) * 1040], 130),
                    d_v[r * 1024:(r + 1) * 1024,
                        130 * j:130 * (j + 1)].rearrange(
                            "(c p) m -> p c m", p=128)))
            kvld[j] = toks_kv

            for qt in range(2):
                exp_ms = {}
                q0 = qt * 512
                expb = [hbuf[:, H_EA + par * 1024: H_EA + (par + 1) * 1024]
                        for par in range(2)]

                def mm2_pair(kc):
                    par = kc % 2
                    for (head, Pfe) in ((0, P_c), (1, P_d)):
                        cbase = kc * 130 + head * 65
                        fn = (lambda e, vb=vb, cbase=cbase, Pfe=Pfe,
                              par=par, head=head, kc=kc: e.matmul(
                                  Pfe[0:65, :],
                                  vb[:, cbase:cbase + 65],
                                  expb[par][:, head * 512:(head + 1) * 512],
                                  start=(kc == 0), stop=(kc == 31)))
                        yield fn

                for kc in range(32):
                    par = kc % 2
                    Psc = P_a if par == 0 else P_b
                    if kc == 0:
                        if qt == 0:
                            P.wait("tensor", kvld[j])
                            if j == 0:
                                P.wait("tensor", "act", v_qev)
                            if j >= 2:
                                P.wait("tensor", "vec", feats_evac[(j - 2, 1)])
                        if (j, qt) != (0, 0):
                            prev = (j, 0) if qt == 1 else (j - 1, 1)
                            P.wait("tensor", "vec", feats_evac.get(prev, 0))
                    else:
                        P.wait("tensor", "act", exp_ms.get(kc - 2, 0))
                        if kc >= 2:
                            for fn in mm2_pair(kc - 2):
                                P.op("tensor", fn)
                    v_g = 0
                    for head, rl, rh in ((0, 0, 64), (1, 64, 128)):
                        fn = (lambda e, kb=kb, kc=kc, Psc=Psc, rl=rl, rh=rh,
                              j=j, q0=q0, head=head: e.matmul(
                                  Psc[:, head * 512:(head + 1) * 512],
                                  kb[rl:rh, kc * 128:(kc + 1) * 128],
                                  qbuf[rl:rh, j * T + q0: j * T + q0 + 512],
                                  start=True, stop=True,
                                  tile_position=(rl, 0)))
                        if head == 1:
                            v_g = P.op("tensor", fn, inc="mm")
                        else:
                            P.op("tensor", fn)
                    P.wait("scalar", "mm", v_g)
                    exp_ms[kc] = P.op("scalar", lambda e, par=par, Psc=Psc:
                                      e.activation(out=expb[par][:, :],
                                                   in_=Psc[:, :], func=AF.Exp,
                                                   scale=SCALE), inc="act")
                P.wait("tensor", "act", exp_ms[31])
                v_feats = 0
                fns = list(mm2_pair(30)) + list(mm2_pair(31))
                for idx, fn in enumerate(fns):
                    if idx == len(fns) - 1:
                        v_feats = P.op("tensor", fn, inc="mm")
                    else:
                        P.op("tensor", fn)
                P.wait("vector", "mm", v_feats)
                P.op("vector", lambda e: e.reciprocal(
                    recA[0:1, :], P_c[64:65, :]), inc="vec")
                P.wait("vector", "vec", C["vec"])
                v_rec = P.op("vector", lambda e: e.reciprocal(
                    recB[0:1, :], P_d[64:65, :]), inc="vec")
                P.wait("vector", "vec", C["vec"])
                P.wait("tensor", "vec", v_rec)
                P.op("tensor", lambda e: e.matmul(
                    P_a[0:64, 0:512], ones64f[0:1, :], recA[0:1, :],
                    start=True, stop=True))
                v_bc = P.op("tensor", lambda e: e.matmul(
                    P_b[0:64, 0:512], ones64f[0:1, :], recB[0:1, :],
                    start=True, stop=True), inc="mm")
                P.wait("vector", "mm", v_bc)
                if j == 0 and qt == 0:
                    P.wait("vector", toks_w2st)
                fcol = j * T + q0
                P.op("vector", lambda e: e.tensor_copy(
                    bcbufA[:, :], P_a[0:64, 0:512]), inc="vec")
                P.op("vector", lambda e, fcol=fcol: e.tensor_tensor(
                    out=hbuf[0:64, H_FE + fcol:H_FE + fcol + 512],
                    in0=P_c[0:64, :], in1=bcbufA[:, :], op=ALU.mult), inc="vec")
                P.op("vector", lambda e: e.tensor_copy(
                    bcbufB[:, :], P_b[0:64, 0:512]), inc="vec")
                vfe = P.op("vector", lambda e, fcol=fcol: e.tensor_tensor(
                    out=hbuf[64:128, H_FE + fcol:H_FE + fcol + 512],
                    in0=P_d[0:64, :], in1=bcbufB[:, :], op=ALU.mult), inc="vec")
                feats_evac[(j, qt)] = vfe
                v_feats_evac_last = vfe

        v_dbg_fe = None
        if DEBUG:
            P.wait("sync", "vec", v_feats_evac_last)
            v_dbg_fe = [P.dma("sync", o_fe[:, :], hbuf[:, H_FE:H_FE + NCH * T]),
                        P.dma("sync", o_q[:, :], qbuf[:, :])]

        wo_band_free = {0: ("mm", 0), 1: ("mm", 0)}
        wo_psum_free = {0: ("vec", v_feats_evac_last), 1: ("vec", v_feats_evac_last)}
        v_woev = {}
        v_womm_last = 0
        for jd in range(16):
            p = jd % 2
            wb = (wbufA if p == 0 else wbufB)[:, 0:2048]
            P.wait("gpsimd", wo_band_free[p][0], wo_band_free[p][1])
            vw = P.dma("gpsimd", chunked(wb, 128), band_ap(wo_d, jd))
            P.wait("tensor", vw)
            P.wait("tensor", wo_psum_free[p][0], wo_psum_free[p][1])
            if jd == 0:
                P.wait("tensor", "vec", v_feats_evac_last)
                if DEBUG:
                    P.wait("tensor", v_dbg_fe)
            P_x = P_a if p == 0 else P_b
            for hh in range(2):
                for i in range(16):
                    fe = H_FE + i * T + hh * 512
                    fn = lambda e, wb=wb, i=i, hh=hh, P_x=P_x, fe=fe: e.matmul(
                        P_x[:, hh * 512:(hh + 1) * 512],
                        wb[:, i * 128:(i + 1) * 128],
                        hbuf[:, fe:fe + 512],
                        start=(i == 0), stop=(i == 15))
                    if hh == 1 and i == 15:
                        v_womm = P.op("tensor", fn, inc="mm")
                    else:
                        P.op("tensor", fn)
            wo_band_free[p] = ("mm", v_womm)
            v_womm_last = v_womm
            P.wait("scalar", "mm", v_womm)
            vv = P.op("scalar", lambda e, P_x=P_x, jd=jd: e.activation(
                out=qbuf[:, jd * T:(jd + 1) * T], in_=P_x[:, :],
                func=AF.Identity, bias=s_bo[:, jd:jd + 1], scale=1.0), inc="act")
            v_woev[jd] = vv
            wo_psum_free[p] = ("act", vv)
        v_woev_last = v_woev[15]

        w1_free = {0: ("mm", 0), 1: ("mm", 0)}
        h_psum_free = {0: ("act", 0), 1: ("act", 0)}
        f2_psum_free = ("act", 0)
        v_fb_last = C["vec"]
        fa_slot_free = {sl: ("vec", v_fb_last) for sl in range(4)}
        fb_slot_free = {sl: ("vec", v_fb_last) for sl in range(4)}
        pbf_free = {0: ("mm", 0), 1: ("mm", 0)}
        pf2_free = {0: pf_free[0], 1: pf_free[1]}
        fout_free = {0: None, 1: None}
        w2_free = {0: ("mm", 0), 1: ("mm", 0)}
        v_out_dma = 0
        first_gelu = True
        for th in range(2):
            t0 = th * 512
            v_hmm = {}
            v_gelu = {}
            for sb_ in range(32):
                p = sb_ % 2
                wb = (wbufA if p == 0 else wbufB)[:, 0:4096]
                qeng = "sync" if sb_ % 2 == 0 else "gpsimd"
                P.wait(qeng, w1_free[p][0], w1_free[p][1])
                if th == 0 and sb_ < 2:
                    P.wait(qeng, "cw1", 1)
                    P.wait(qeng, "mm", v_womm_last)
                toks_w1l = [P.dma(
                    qeng,
                    wb.rearrange("p (i c) -> p i c", c=256),
                    d_w1t[:, sb_ * 256:(sb_ + 1) * 256].rearrange(
                        "(i p) c -> p i c", p=128))]
                for l in range(2):
                    bb = sb_ * 2 + l
                    pp = bb % 2
                    P_x = P_c if pp == 0 else P_d
                    P.wait("tensor", toks_w1l)
                    P.wait("tensor", h_psum_free[pp][0], h_psum_free[pp][1])
                    if th == 0 and bb == 0:
                        P.wait("tensor", "vec", v_feats_evac_last)
                    for i in range(16):
                        fn = (lambda e, wb=wb, i=i, l=l, P_x=P_x, t0=t0: e.matmul(
                            P_x[:, :],
                            wb[:, i * 256 + l * 128: i * 256 + (l + 1) * 128],
                            xnbuf[:, i * T + t0: i * T + t0 + 512],
                            start=(i == 0), stop=(i == 15)))
                        if i == 15:
                            v_hmm[bb] = P.op("tensor", fn, inc="mm")
                        else:
                            P.op("tensor", fn)
                    P.wait("scalar", "mm", v_hmm[bb])
                    if DEBUG and th == 1 and bb == 0:
                        P.wait("scalar", v_dbg_h)
                    if first_gelu:
                        P.wait("scalar", "mm", v_womm_last)
                        first_gelu = False
                    v_gelu[bb] = P.op("scalar", lambda e, P_x=P_x, bb=bb: e.activation(
                        out=hbuf[:, H_H + bb * 512:H_H + (bb + 1) * 512],
                        in_=P_x[:, :], func=AF.Gelu, bias=s_b1[:, bb:bb + 1],
                        scale=1.0), inc="act")
                    h_psum_free[pp] = ("act", v_gelu[bb])
                w1_free[p] = ("mm", v_hmm[sb_ * 2 + 1])
            v_h_done = v_gelu[63]
            if DEBUG and th == 0:
                P.wait("sync", "act", v_h_done)
                v_dbg_h = P.dma("sync", o_h[:, :], hbuf[:, H_H:H_H + 32768])
            f2slots = [P_a[:, 0:512], P_a[:, 512:1024],
                       P_b[:, 0:512], P_b[:, 512:1024]]
            for g in range(4):
                v_f2 = {}
                for fg in range(16):
                    p = fg % 2
                    w2b = (wbufA if p == 0 else wbufB)[:, 4096:6144]
                    P.wait("sync" if p == 0 else "gpsimd",
                           w2_free[p][0], w2_free[p][1])
                    if th == 0 and g == 0 and fg < 2:
                        P.wait("sync" if p == 0 else "gpsimd", "cw2", 1)
                    vld = P.dma(
                        "sync" if p == 0 else "gpsimd",
                        w2b.rearrange("p (f c) -> p f c", c=512),
                        d_w2t[fg * 512:(fg + 1) * 512,
                              g * 512:(g + 1) * 512].rearrange(
                                  "(f p) c -> p f c", p=128))
                    P.wait("tensor", vld)
                    if fg == 0:
                        P.wait("tensor", "act", v_h_done)
                        P.wait("tensor", f2_psum_free[0], f2_psum_free[1])
                        if th == 0 and g == 0:
                            P.wait("tensor", "act", v_woev_last)
                    for fl in range(4):
                        fbk = fg * 4 + fl
                        for sl in range(4):
                            fn = (lambda e, w2b=w2b, sl=sl, fl=fl, fbk=fbk:
                                  e.matmul(
                                      f2slots[sl],
                                      w2b[:, fl * 512 + sl * 128:
                                          fl * 512 + (sl + 1) * 128],
                                      hbuf[:, H_H + fbk * 512:
                                           H_H + (fbk + 1) * 512],
                                      start=(fbk == 0), stop=(fbk == 63)))
                            if sl == 3 and fl == 3:
                                v_f2[fg] = P.op("tensor", fn, inc="mm")
                            else:
                                P.op("tensor", fn)
                    w2_free[p] = ("mm", v_f2[fg])
                v_last_evac = 0
                for sl in range(4):
                    jd = g * 4 + sl
                    ppar = jd % 2
                    P.wait("scalar", "mm", v_f2[15])
                    P.wait("scalar", fa_slot_free[sl][0], fa_slot_free[sl][1])
                    v_silu = P.op("scalar", lambda e, sl=sl, jd=jd: e.activation(
                        out=fa[:, sl * 512:(sl + 1) * 512], in_=f2slots[sl],
                        func=AF.Silu, bias=s_b2[:, jd:jd + 1], scale=1.0),
                        inc="act")
                    v_last_evac = v_silu
                    P.wait("vector", "act", v_silu)
                    if th == 0 and g == 0:
                        P.wait("vector", "act", v_woev_last)
                    P.wait("vector", pbf_free[ppar][0], pbf_free[ppar][1])
                    v_p = P.op("vector", lambda e, sl=sl, jd=jd, t0=t0, ppar=ppar:
                               e.tensor_tensor(
                                   out=hbuf[:, H_PBF + ppar * 512:
                                            H_PBF + (ppar + 1) * 512],
                                   in0=fa[:, sl * 512:(sl + 1) * 512],
                                   in1=qbuf[:, jd * T + t0: jd * T + t0 + 512],
                                   op=ALU.mult), inc="vec")
                    fa_slot_free[sl] = ("vec", v_p)
                    P.wait("tensor", "vec", v_p)
                    P.wait("tensor", pf2_free[ppar][0], pf2_free[ppar][1])
                    v_tp5 = 0
                    for s4 in range(4):
                        fn = (lambda e, s4=s4, ppar=ppar: e.transpose(
                            P_f[:, ppar * 1024 + s4 * 128:
                                ppar * 1024 + (s4 + 1) * 128],
                            hbuf[:, H_PBF + ppar * 512 + s4 * 128:
                                 H_PBF + ppar * 512 + (s4 + 1) * 128],
                            ident[:, :]))
                        if s4 == 3:
                            v_tp5 = P.op("tensor", fn, inc="mm")
                        else:
                            P.op("tensor", fn)
                    pbf_free[ppar] = ("mm", v_tp5)
                    P.wait("sync", fb_slot_free[sl][0], fb_slot_free[sl][1])
                    vxl = P.dma("sync",
                                chunked(fb[:, sl * 512:(sl + 1) * 512], 128),
                                x_d[th * 512:(th + 1) * 512,
                                    jd * 128:(jd + 1) * 128].rearrange(
                                        "(s p) d -> p s d", p=128))
                    P.wait("vector", "mm", v_tp5)
                    P.wait("vector", vxl)
                    P.wait("vector", fout_free[ppar])
                    v_ot = P.op("vector", lambda e, sl=sl, ppar=ppar:
                                e.tensor_tensor(
                                    out=fout[:, ppar * 512:(ppar + 1) * 512],
                                    in0=P_f[:, ppar * 1024:ppar * 1024 + 512],
                                    in1=fb[:, sl * 512:(sl + 1) * 512],
                                    op=ALU.add), inc="vec")
                    pf2_free[ppar] = ("vec", v_ot)
                    fb_slot_free[sl] = ("vec", v_ot)
                    P.wait("sync", "vec", v_ot)
                    v_out_dma = P.dma(
                        "sync",
                        out_d[th * 512:(th + 1) * 512,
                              jd * 128:(jd + 1) * 128].rearrange(
                                  "(s p) d -> p s d", p=128),
                        chunked(fout[:, ppar * 512:(ppar + 1) * 512], 128))
                    fout_free[ppar] = v_out_dma
                f2_psum_free = ("act", v_last_evac)

        if DEBUG:
            P.wait("sync", "vec", C["vec"])
            P.wait("sync", "act", C["act"])
            P.dma("sync", o_xn[:, :], xnbuf[:, :])
            P.dma("sync", o_st[:, :], qbuf[:, :])
            P.dma("sync", o_sum[:, :], ssmall[0:1, 2:4])
            P.dma("sync", o_kg[:, :], d_k[:, :])
            P.dma("sync", o_vg[:, :], d_v[:, :])
            for r in range(16):
                P.dma("sync", o_w1t[r * 128:(r + 1) * 128, :],
                      d_w1t[r * 128:(r + 1) * 128, :])
            for r in range(64):
                P.dma("sync", o_w2t[r * 128:(r + 1) * 128, :],
                      d_w2t[r * 128:(r + 1) * 128, :])

        for s in SQ:
            P.wait("sync", s, C[s])

        with nc.Block() as block:
            @block.sync
            def _(e):
                for fn in P.items["sync"]:
                    fn(e)

            @block.gpsimd
            def _(e):
                for fn in P.items["gpsimd"]:
                    fn(e)

            @block.tensor
            def _(e):
                for fn in P.items["tensor"]:
                    fn(e)

            @block.vector
            def _(e):
                for fn in P.items["vector"]:
                    fn(e)

            @block.scalar
            def _(e):
                for fn in P.items["scalar"]:
                    fn(e)

    return nc


_NC_CACHE = None


def kernel(L0_attn_x, L0_attn_Wq, L0_attn_bq, L0_attn_Wk, L0_attn_bk,
           L0_attn_Wv, L0_attn_bv, L0_attn_Wo, L0_attn_bo,
           L0_ffn_w1, L0_ffn_b1, L0_ffn_w2, L0_ffn_b2):
    global _NC_CACHE
    x = np.ascontiguousarray(np.asarray(L0_attn_x, dtype=np.float32))
    wq = np.ascontiguousarray(np.asarray(L0_attn_Wq, dtype=np.float32))
    wk = np.ascontiguousarray(np.asarray(L0_attn_Wk, dtype=np.float32))
    wv = np.ascontiguousarray(np.asarray(L0_attn_Wv, dtype=np.float32))
    wo = np.ascontiguousarray(np.asarray(L0_attn_Wo, dtype=np.float32))
    w1 = np.ascontiguousarray(np.asarray(L0_ffn_w1, dtype=np.float32))
    w2 = np.ascontiguousarray(np.asarray(L0_ffn_w2, dtype=np.float32))

    def fold(b, n):
        return np.ascontiguousarray(
            np.asarray(b, dtype=np.float32).reshape(n, 128).T)

    bq2 = fold(L0_attn_bq, 16)
    bk2 = fold(L0_attn_bk, 16)
    bo2 = fold(L0_attn_bo, 16)
    b12 = fold(L0_ffn_b1, 64)
    b22 = fold(L0_ffn_b2, 16)
    bvr = np.ascontiguousarray(
        np.asarray(L0_attn_bv, dtype=np.float32).reshape(1, D))

    if _NC_CACHE is None:
        _NC_CACHE = build_program()
    nc = _NC_CACHE

    in_maps = []
    for c in range(8):
        b, sl = c // 4, c % 4
        in_maps.append({
            "x": np.ascontiguousarray(x[b, sl * T:(sl + 1) * T, :]),
            "wq": wq, "wk": wk, "wv": wv, "wo": wo,
            "bq": bq2, "bk": bk2, "bo": bo2, "bv_row": bvr,
            "b1": b12, "b2": b22,
            "w1s": np.ascontiguousarray(w1[:, c * 256:(c + 1) * 256]),
            "w2s": np.ascontiguousarray(w2[:, c * 1024:(c + 1) * 1024]),
        })
    res = run_bass_kernel_spmd(nc, in_maps, core_ids=list(range(8)))
    out = np.empty((2, 4096, D), dtype=np.float32)
    for c in range(8):
        b, sl = c // 4, c % 4
        out[b, sl * T:(sl + 1) * T, :] = res.results[c]["out"]
    return out


# version 05
# speedup vs baseline: 1.5020x; 1.0794x over previous
import numpy as np
import ml_dtypes
import concourse.bass as bass
import concourse.mybir as mybir
from concourse.bass_utils import run_bass_kernel_spmd

F32 = mybir.dt.float32
BF16 = mybir.dt.bfloat16
AF = mybir.ActivationFunctionType
ALU = mybir.AluOpType
AX = mybir.AxisListType
bf = ml_dtypes.bfloat16

D = 2048
DFF = 8192
T = 1024
NCH = 16
PAIRS = 16
SCALE = float(1.0 / np.sqrt(32.0))
BIG = float(2.0 ** 63)
HALFBIG = float(0.5 * 2.0 ** 63)
NMAT = float(DFF * D)
GROUPS_KV = [[0, 1, 2, 3], [4, 5, 6, 7]]
GROUP_ALL = [[0, 1, 2, 3, 4, 5, 6, 7]]

DEBUG = False

SEM = {}
NSQ = 8
NGQ = 4
SQ = tuple(f"sq{i}" for i in range(NSQ))
GQ = tuple(f"gq{i}" for i in range(NGQ))
SEMNAMES = SQ + GQ + ("mm", "act", "vec", "gp",
                      "ccr", "cck", "ccv", "cw1", "cw2")


class Plan:

    ENGS = ("sync", "gpsimd", "tensor", "vector", "scalar")

    def __init__(self):
        self.items = {e: [] for e in self.ENGS}
        self.ctr = {s: 0 for s in SEMNAMES}
        self.rr = {"sync": 0, "gpsimd": 0}

    def add(self, eng, fn):
        self.items[eng].append(fn)

    def wait(self, eng, sem, val=None):
        if val is None:
            tok = sem
            if tok is None:
                return
            if isinstance(tok, list):
                best = {}
                for s, v in tok:
                    if s is not None and v > best.get(s, 0):
                        best[s] = v
                for s, v in best.items():
                    self.add(eng, lambda e, s=s, v=v: e.wait_ge(SEM[s], v))
                return
            sem, val = tok
        if sem is None or val is None or val <= 0:
            return
        self.add(eng, lambda e, s=sem, v=val: e.wait_ge(SEM[s], v))

    def dma(self, eng, out_ap, in_ap, **kw):
        pool = SQ if eng == "sync" else GQ
        s = pool[self.rr[eng] % len(pool)]
        self.rr[eng] += 1
        self.ctr[s] += 16
        v = self.ctr[s]
        self.add(eng, lambda e, o=out_ap, i=in_ap, kw=dict(kw), s=s: e.dma_start(
            out=o, in_=i, **kw).then_inc(SEM[s], 16))
        return (s, v)

    def op(self, eng, fn, inc=None):
        if inc is None:
            self.add(eng, fn)
            return None
        self.ctr[inc] += 1
        v = self.ctr[inc]
        self.add(eng, lambda e, f=fn, s=inc: f(e).then_inc(SEM[s], 1))
        return v

    def collective(self, sem, kind, op_, groups, in_ap, out_ap):
        self.ctr[sem] += 1
        v = self.ctr[sem]
        self.add("gpsimd", lambda e, k=kind, o=op_, g=groups, i=in_ap, oo=out_ap,
                 s=sem: e.collective_compute(
                     k, o, replica_groups=g, ins=[i.opt()],
                     outs=[oo.opt()]).then_inc(SEM[s], 1))
        return v


def build_program():
    nc = bass.Bass("TRN2", target_bir_lowering=False)

    x_d = nc.dram_tensor("x", [T, D], F32, kind="ExternalInput").ap()
    wq_d = nc.dram_tensor("wq", [D, D], F32, kind="ExternalInput").ap()
    wk_d = nc.dram_tensor("wk", [D, D], F32, kind="ExternalInput").ap()
    wv_d = nc.dram_tensor("wv", [D, D], F32, kind="ExternalInput").ap()
    wo_d = nc.dram_tensor("wo", [D, D], F32, kind="ExternalInput").ap()
    bq_d = nc.dram_tensor("bq", [128, 16], F32, kind="ExternalInput").ap()
    bk_d = nc.dram_tensor("bk", [128, 16], F32, kind="ExternalInput").ap()
    bo_d = nc.dram_tensor("bo", [128, 16], F32, kind="ExternalInput").ap()
    bv_row_d = nc.dram_tensor("bv_row", [1, D], F32, kind="ExternalInput").ap()
    b1_d = nc.dram_tensor("b1", [128, 64], F32, kind="ExternalInput").ap()
    b2_d = nc.dram_tensor("b2", [128, 16], F32, kind="ExternalInput").ap()
    w1s_d = nc.dram_tensor("w1s", [DFF, 256], F32, kind="ExternalInput").ap()
    w2s_d = nc.dram_tensor("w2s", [D, 1024], F32, kind="ExternalInput").ap()
    out_d = nc.dram_tensor("out", [T, D], F32, kind="ExternalOutput").ap()
    if DEBUG:
        o_xn = nc.dram_tensor("o_xn", [128, NCH * T], BF16,
                              kind="ExternalOutput").ap()
        o_st = nc.dram_tensor("o_st", [128, NCH * T], BF16,
                              kind="ExternalOutput").ap()
        o_fe = nc.dram_tensor("o_fe", [128, NCH * T], BF16,
                              kind="ExternalOutput").ap()
        o_h = nc.dram_tensor("o_h", [128, 32768], BF16,
                             kind="ExternalOutput").ap()
        o_kg = nc.dram_tensor("o_kg", [4 * D, T], BF16,
                              kind="ExternalOutput").ap()
        o_vg = nc.dram_tensor("o_vg", [4 * T, 2080], BF16,
                              kind="ExternalOutput").ap()
        o_w1t = nc.dram_tensor("o_w1t", [D, DFF], BF16,
                               kind="ExternalOutput").ap()
        o_w2t = nc.dram_tensor("o_w2t", [DFF, D], BF16,
                               kind="ExternalOutput").ap()
        o_sum = nc.dram_tensor("o_sum", [1, 2], F32, kind="ExternalOutput").ap()
        o_q = nc.dram_tensor("o_q", [128, NCH * T], BF16,
                             kind="ExternalOutput").ap()

    d_k_in = nc.dram_tensor("d_k_in", [D, T], BF16).ap()
    d_k = nc.dram_tensor("d_k", [4 * D, T], BF16).ap()
    d_v_in = nc.dram_tensor("d_v_in", [T, 2080], BF16).ap()
    d_v = nc.dram_tensor("d_v", [4 * T, 2080], BF16).ap()
    d_w1t_in = nc.dram_tensor("d_w1t_in", [256, DFF], BF16).ap()
    d_w1t = nc.dram_tensor("d_w1t", [D, DFF], BF16).ap()
    d_w2t_in = nc.dram_tensor("d_w2t_in", [1024, D], BF16).ap()
    d_w2t = nc.dram_tensor("d_w2t", [DFF, D], BF16).ap()
    d_sum_in = nc.dram_tensor("d_sum_in", [1, 2], F32).ap()
    d_sum = nc.dram_tensor("d_sum", [1, 2], F32).ap()

    from contextlib import ExitStack
    ctx = ExitStack()
    with ctx:
        def sbt(name, shape, dt):
            return ctx.enter_context(nc.sbuf_tensor(name, shape, dt))

        def pst(name, shape, dt):
            return ctx.enter_context(nc.psum_tensor(name, shape, dt))

        xnbuf = sbt("xnbuf", [128, NCH * T], BF16)
        qbuf = sbt("qbuf", [128, NCH * T], BF16)
        hbuf = sbt("hbuf", [128, 37120], BF16)
        wbufA = sbt("wbufA", [128, 8192], BF16)
        wbufB = sbt("wbufB", [128, 8192], BF16)
        fa = sbt("fa", [128, 2048], F32)
        fb = sbt("fb", [128, 2048], F32)

        fout = sbt("fout", [128, 1024], F32)
        s_bq = sbt("s_bq", [128, 16], F32)
        s_bk = sbt("s_bk", [128, 16], F32)
        s_bo = sbt("s_bo", [128, 16], F32)
        s_b1 = sbt("s_b1", [128, 64], F32)
        s_b2 = sbt("s_b2", [128, 16], F32)
        s_bvr = sbt("s_bvr", [1, 2048], F32)
        ident = sbt("ident", [128, 128], BF16)
        ones1 = sbt("ones1", [1, 128], F32)
        ones128 = sbt("ones128", [128, 1], F32)
        ones64f = sbt("ones64f", [1, 64], F32)
        stat = sbt("stat", [128, 16], F32)
        epst = sbt("epst", [128, 1], F32)
        acc = sbt("acc", [128, 2], F32)
        acct = sbt("acct", [128, 1], F32)
        ssmall = sbt("ssmall", [1, 8], F32)
        inv_bc = sbt("inv_bc", [128, 4], F32)
        recA = sbt("recA", [1, 512], F32)
        recB = sbt("recB", [1, 512], F32)
        bcbufA = sbt("bcbufA", [64, 512], F32)
        bcbufB = sbt("bcbufB", [64, 512], F32)

        H_XN = 0
        H_KT = 0
        H_VST = 2048
        H_TU, H_TV, H_TT = 0, 2048, 4096
        H_WST = 6144
        H_KB = 0
        H_VB = 8192
        H_EA = 16512
        H_EB = 18560
        H_FE = 20608
        H_H = 0
        H_PBF = 32768

        P_a = pst("P_a", [128, 1024], F32)
        P_b = pst("P_b", [128, 1024], F32)
        P_c = pst("P_c", [128, 512], F32)
        P_d = pst("P_d", [128, 512], F32)
        P_f = pst("P_f", [128, 2048], BF16)

        SEM.clear()
        for s in SEMNAMES:
            SEM[s] = ctx.enter_context(nc.semaphore(s))

        P = Plan()
        C = P.ctr

        P.op("gpsimd", lambda e: e.memset(ident[:, :], 0.0))
        v_id = P.op("gpsimd", lambda e: e.affine_select(
            out=ident[:, :], in_=ident[:, :], compare_op=ALU.not_equal,
            fill=1.0, base=0, pattern=[[-1, 128]], channel_multiplier=1),
            inc="gp")
        P.op("gpsimd", lambda e: e.memset(ones1[:, :], 1.0))
        P.op("gpsimd", lambda e: e.memset(ones128[:, :], 1.0))
        P.op("gpsimd", lambda e: e.memset(ones64f[:, :], 1.0))
        v_ones = P.op("gpsimd", lambda e: e.memset(epst[:, :], 1e-5), inc="gp")

        toks_bias = [P.dma("sync", s_bq[:, :], bq_d[:, :]),
                     P.dma("sync", s_bk[:, :], bk_d[:, :]),
                     P.dma("sync", s_bo[:, :], bo_d[:, :]),
                     P.dma("sync", s_b1[:, :], b1_d[:, :]),
                     P.dma("sync", s_b2[:, :], b2_d[:, :])]
        v_bias = P.dma("sync", s_bvr[:, :], bv_row_d[:, :])

        v_acc0 = P.op("vector", lambda e: e.memset(acc[:, :], 0.0), inc="vec")
        fa_free = ("vec", 0)
        fb_free = ("vec", 0)
        for which, src_ap, ntile, ng, tw in ((0, w1s_d, 8, 8, 256),
                                             (1, w2s_d, 8, 2, 1024)):
            rows = ng * 128
            for k in range(ntile):
                buf, free = (fa, fa_free) if k % 2 == 0 else (fb, fb_free)
                P.wait("sync", free[0], free[1])
                vl = P.dma("sync", buf[:, :].rearrange("p (g c) -> p g c", c=tw),
                           src_ap[k * rows:(k + 1) * rows, :].rearrange(
                               "(g p) c -> p g c", p=128))
                P.wait("vector", vl)
                P.op("vector", lambda e, buf=buf: e.reduce_sum(
                    acct[:, 0:1], buf[:, :], axis=AX.X,
                    apply_absolute_value=True), inc="vec")
                if k % 2 == 0:
                    fa_free = ("vec", C["vec"])
                else:
                    fb_free = ("vec", C["vec"])
                P.wait("vector", "vec", C["vec"])
                P.op("vector", lambda e, w=which: e.tensor_add(
                    acc[:, w:w + 1], acc[:, w:w + 1], acct[:, 0:1]), inc="vec")
        v_abs_done = C["vec"]
        P.wait("tensor", "vec", v_abs_done)
        P.wait("tensor", "gp", v_ones)
        v_sum_mm = P.op("tensor", lambda e: e.matmul(
            P_c[0:1, 0:2], ones128[:, 0:1], acc[:, :], start=True, stop=True),
            inc="mm")
        P.wait("vector", "mm", v_sum_mm)
        v_sum_cp = P.op("vector", lambda e: e.tensor_copy(
            ssmall[0:1, 0:2], P_c[0:1, 0:2]), inc="vec")
        P.wait("sync", "vec", v_sum_cp)
        v_sum_st = P.dma("sync", d_sum_in[:, :], ssmall[0:1, 0:2])
        P.wait("gpsimd", v_sum_st)
        v_ar = P.collective("ccr", "AllReduce", ALU.add, GROUP_ALL, d_sum_in, d_sum)

        xnbf = hbuf[:, H_XN:H_XN + 2048]
        xnbf_free = ("mm", 0)
        pf_evac = {0: 0, 1: 0}
        v_tph = 0
        xin_free = {0: fa_free}
        stat_free = {}
        for i in range(8):
            xin = fa[:, :]
            sb0 = (i % 2) * 8
            P.wait("sync", xin_free[0][0], xin_free[0][1])
            vl = P.dma("sync", xin, x_d[i * 128:(i + 1) * 128, :])
            P.wait("vector", vl)
            P.wait("vector", "act", stat_free.get(i % 2, 0))
            P.op("vector", lambda e, xin=xin, sb0=sb0: e.reduce_sum(
                stat[:, sb0:sb0 + 1], xin, axis=AX.X, negate=True), inc="vec")
            P.wait("vector", "vec", C["vec"])
            v_nm = P.op("vector", lambda e, sb0=sb0: e.tensor_scalar_mul(
                stat[:, sb0 + 1:sb0 + 2], stat[:, sb0:sb0 + 1],
                1.0 / 2048.0), inc="vec")
            P.wait("scalar", "vec", v_nm)
            P.op("scalar", lambda e, xin=xin, sb0=sb0: e.activation(
                out=fb[:, :], in_=xin, func=AF.Identity,
                bias=stat[:, sb0 + 1:sb0 + 2], scale=1.0), inc="act")
            P.wait("scalar", "act", C["act"])
            v_sqr = P.op("scalar", lambda e, xin=xin, sb0=sb0: e.activation(
                out=xin, in_=fb[:, :], func=AF.Square,
                accum_out=stat[:, sb0 + 2:sb0 + 3]), inc="act")
            xin_free[0] = ("act", v_sqr)
            P.wait("scalar", "act", C["act"])
            if i == 0:
                P.wait("scalar", "gp", v_ones)
            v_std = P.op("scalar", lambda e, sb0=sb0: e.activation(
                out=stat[:, sb0 + 3:sb0 + 4], in_=stat[:, sb0 + 2:sb0 + 3],
                func=AF.Sqrt, bias=epst[:, 0:1], scale=1.0 / 2048.0), inc="act")
            stat_free[i % 2] = v_std
            P.wait("vector", "act", v_std)
            v_rstd = P.op("vector", lambda e, sb0=sb0: e.reciprocal(
                stat[:, sb0 + 4:sb0 + 5], stat[:, sb0 + 3:sb0 + 4]), inc="vec")
            P.wait("vector", "vec", C["vec"])
            v_rstd2 = P.op("vector", lambda e, sb0=sb0: e.tensor_copy(
                stat[:, sb0 + 5:sb0 + 6], stat[:, sb0 + 4:sb0 + 5]), inc="vec")
            P.wait("scalar", "vec", v_rstd2)
            P.wait("scalar", xnbf_free[0], xnbf_free[1])
            v_xn = P.op("scalar", lambda e, sb0=sb0: e.activation(
                out=xnbf[:, :], in_=fb[:, :], func=AF.Identity,
                scale=stat[:, sb0 + 5:sb0 + 6]), inc="act")
            v_xn_last = v_xn
            P.wait("tensor", "act", v_xn)
            if i == 0:
                P.wait("tensor", "gp", v_id)
            for half in range(2):
                P.wait("tensor", "vec", pf_evac[half])
                for k in range(8):
                    kk = half * 8 + k
                    fn = lambda e, kk=kk, k=k, half=half: e.transpose(
                        P_f[:, half * 1024 + k * 128: half * 1024 + k * 128 + 128],
                        xnbf[:, kk * 128:kk * 128 + 128], ident[:, :])
                    if k == 7:
                        v_tph = P.op("tensor", fn, inc="mm")
                    else:
                        P.op("tensor", fn)
                P.wait("vector", "mm", v_tph)
                out3 = xnbuf[:, half * 8 * T: (half * 8 + 8) * T].rearrange(
                    "p (c t) -> p c t", t=T)[:, :, i * 128:i * 128 + 128]
                in3 = P_f[:, half * 1024: half * 1024 + 1024].rearrange(
                    "p (c t) -> p c t", t=128)
                pf_evac[half] = P.op(
                    "vector", lambda e, o=out3, ii=in3: e.tensor_copy(o, ii),
                    inc="vec")
            xnbf_free = ("mm", v_tph)
        fa_free = xin_free[0]
        v_xnT_done = max(pf_evac.values())

        def band_ap(w_ap, jd):
            return w_ap[:, jd * 128:(jd + 1) * 128].rearrange(
                "(i p) d -> p i d", p=128)

        def chunked(sb_ap, inner):
            return sb_ap.rearrange("p (i d) -> p i d", d=inner)

        band_free = {0: ("gp", 0), 1: ("gp", 0)}
        psum_free = {0: ("act", 0), 1: ("act", 0)}
        kst_free = {0: None, 1: None}
        toks_kst = []
        v_kmm_last = 0
        for jd in range(16):
            p = jd % 2
            wb = (wbufA if p == 0 else wbufB)[:, 0:2048]
            P.wait("gpsimd", band_free[p][0], band_free[p][1])
            vw = P.dma("gpsimd", chunked(wb, 128), band_ap(wk_d, jd))
            P.wait("tensor", vw)
            P.wait("tensor", psum_free[p][0], psum_free[p][1])
            if jd == 0:
                P.wait("tensor", "vec", v_xnT_done)
            P_x = P_a if p == 0 else P_b
            for hh in range(2):
                for i in range(16):
                    fn = lambda e, wb=wb, i=i, hh=hh, P_x=P_x: e.matmul(
                        P_x[:, hh * 512:(hh + 1) * 512],
                        wb[:, i * 128:(i + 1) * 128],
                        xnbuf[:, i * T + hh * 512: i * T + (hh + 1) * 512],
                        start=(i == 0), stop=(i == 15))
                    if hh == 1 and i == 15:
                        v_kmm = P.op("tensor", fn, inc="mm")
                    else:
                        P.op("tensor", fn)
            band_free[p] = ("mm", v_kmm)
            v_kmm_last = v_kmm
            P.wait("scalar", "mm", v_kmm)
            P.wait("scalar", kst_free[p])
            if jd == 0:
                P.wait("scalar", toks_bias)
            if jd < 2:
                P.wait("scalar", "mm", v_tph)
            kts = hbuf[:, H_KT + p * 1024: H_KT + (p + 1) * 1024]
            v_kev = P.op("scalar", lambda e, P_x=P_x, jd=jd, kts=kts: e.activation(
                out=kts, in_=P_x[:, :], func=AF.Identity,
                bias=s_bk[:, jd:jd + 1], scale=1.0), inc="act")
            psum_free[p] = ("act", v_kev)
            P.wait("sync", "act", v_kev)
            v_kst = P.dma("sync", d_k_in[jd * 128:(jd + 1) * 128, :], kts)
            toks_kst.append(v_kst)
            kst_free[p] = v_kst
        P.wait("gpsimd", toks_kst)
        v_agk = P.collective("cck", "AllGather", ALU.bypass, GROUPS_KV, d_k_in, d_k)

        P.wait("tensor", v_bias)
        P.wait("tensor", "gp", v_ones)
        P.wait("tensor", "vec", v_sum_cp)
        v_bvbc = 0
        for q in range(4):
            vq = P.op("tensor", lambda e, q=q: e.matmul(
                P_c[:, :], ones1[0:1, :], s_bvr[0:1, q * 512:(q + 1) * 512],
                start=True, stop=True), inc="mm")
            P.wait("vector", "mm", vq)
            v_bvbc = P.op("vector", lambda e, q=q: e.tensor_copy(
                fa[:, q * 512:(q + 1) * 512], P_c[:, :]), inc="vec")
            P.wait("tensor", "vec", v_bvbc)

        P.wait("gpsimd", "vec", v_xnT_done)
        v_vst_ms = 0
        for tt in range(8):
            v_vst_ms = P.op("gpsimd", lambda e, tt=tt: e.memset(
                hbuf[:, H_VST + tt * 2080: H_VST + (tt + 1) * 2080], 1.0),
                inc="gp")
        v_vev = {}
        vband_free = {0: band_free[0], 1: band_free[1]}
        vpsum_free = {0: ("vec", v_bvbc), 1: ("vec", v_bvbc)}
        for q in range(4):
            p = q % 2
            wb = wbufA if p == 0 else wbufB
            P.wait("gpsimd", vband_free[p][0], vband_free[p][1])
            vw = P.dma("gpsimd", chunked(wb[:, :], 512),
                       wv_d[:, q * 512:(q + 1) * 512].rearrange(
                           "(i p) d -> p i d", p=128))
            v_vmm = 0
            for tt in range(8):
                pp = tt % 2
                P_x = P_c if pp == 0 else P_d
                P.wait("tensor", vw)
                P.wait("tensor", vpsum_free[pp][0], vpsum_free[pp][1])
                for i in range(16):
                    fn = lambda e, i=i, wb=wb, tt=tt, P_x=P_x: e.matmul(
                        P_x[:, :],
                        xnbuf[:, i * T + tt * 128: i * T + tt * 128 + 128],
                        wb[:, i * 512:(i + 1) * 512],
                        start=(i == 0), stop=(i == 15))
                    if i == 15:
                        v_vmm = P.op("tensor", fn, inc="mm")
                    else:
                        P.op("tensor", fn)
                P.wait("vector", "mm", v_vmm)
                if q == 0 and tt == 0:
                    P.wait("vector", "gp", v_vst_ms)
                base = H_VST + tt * 2080 + q * 520
                o3 = hbuf[:, base: base + 520].rearrange(
                    "p (h e) -> p h e", e=65)[:, :, 0:64]
                i3 = P_x[:, :].rearrange("p (h e) -> p h e", e=64)
                b3 = fa[:, q * 512:(q + 1) * 512].rearrange(
                    "p (h e) -> p h e", e=64)
                vv = P.op("vector", lambda e, o3=o3, i3=i3, b3=b3: e.tensor_tensor(
                    out=o3, in0=i3, in1=b3, op=ALU.add), inc="vec")
                v_vev[(tt, q)] = vv
                vpsum_free[pp] = ("vec", vv)
            vband_free[p] = ("mm", v_vmm)
        toks_vst = []
        for tt in range(8):
            P.wait("sync", "vec", v_vev[(tt, 3)])
            toks_vst.append(P.dma(
                "sync", d_v_in[tt * 128:(tt + 1) * 128, :],
                hbuf[:, H_VST + tt * 2080: H_VST + (tt + 1) * 2080]))
        P.wait("gpsimd", toks_vst)
        v_agv = P.collective("ccv", "AllGather", ALU.bypass, GROUPS_KV, d_v_in, d_v)

        qband_free = {0: vband_free[0], 1: vband_free[1]}
        qpsum_free = {0: psum_free[0], 1: psum_free[1]}
        v_qev = 0
        for jd in range(16):
            p = jd % 2
            wb = (wbufA if p == 0 else wbufB)[:, 0:2048]
            P.wait("gpsimd", qband_free[p][0], qband_free[p][1])
            vw = P.dma("gpsimd", chunked(wb, 128), band_ap(wq_d, jd))
            P.wait("tensor", vw)
            P.wait("tensor", qpsum_free[p][0], qpsum_free[p][1])
            P_x = P_a if p == 0 else P_b
            for hh in range(2):
                for i in range(16):
                    fn = lambda e, wb=wb, i=i, hh=hh, P_x=P_x: e.matmul(
                        P_x[:, hh * 512:(hh + 1) * 512],
                        wb[:, i * 128:(i + 1) * 128],
                        xnbuf[:, i * T + hh * 512: i * T + (hh + 1) * 512],
                        start=(i == 0), stop=(i == 15))
                    if hh == 1 and i == 15:
                        v_qmm = P.op("tensor", fn, inc="mm")
                    else:
                        P.op("tensor", fn)
            qband_free[p] = ("mm", v_qmm)
            P.wait("scalar", "mm", v_qmm)
            v_qev = P.op("scalar", lambda e, P_x=P_x, jd=jd: e.activation(
                out=qbuf[:, jd * T:(jd + 1) * T], in_=P_x[:, :],
                func=AF.Identity, bias=s_bq[:, jd:jd + 1], scale=1.0), inc="act")
            qpsum_free[p] = ("act", v_qev)

        P.wait("sync", "ccr", v_ar)
        v_sumld = P.dma("sync", ssmall[0:1, 2:4], d_sum[:, :])
        P.wait("vector", v_sumld)
        P.op("vector", lambda e: e.tensor_scalar(
            out=ssmall[0:1, 4:6], in0=ssmall[0:1, 2:4], scalar1=1.0 / NMAT,
            scalar2=1e-8, op0=ALU.mult, op1=ALU.max), inc="vec")
        P.wait("vector", "vec", C["vec"])
        v_inv = P.op("vector", lambda e: e.reciprocal(
            ssmall[0:1, 6:8], ssmall[0:1, 4:6]), inc="vec")
        P.wait("vector", "vec", C["vec"])
        P.wait("tensor", "vec", v_inv)
        P.wait("tensor", vpsum_free[1][0], vpsum_free[1][1])
        v_invb = P.op("tensor", lambda e: e.matmul(
            P_d[:, 0:2], ones1[0:1, :], ssmall[0:1, 6:8],
            start=True, stop=True), inc="mm")
        P.wait("vector", "mm", v_invb)
        P.op("vector", lambda e: e.tensor_scalar_mul(
            inv_bc[:, 0:2], P_d[:, 0:2], BIG), inc="vec")
        v_invbc = P.op("vector", lambda e: e.tensor_scalar_mul(
            inv_bc[:, 2:4], P_d[:, 0:2], -BIG), inc="vec")

        v_wst = {}
        pf_free = {0: ("vec", pf_evac[0]), 1: ("vec", pf_evac[1])}
        tern_free = ("mm", 0)
        v_w1std = v_w2std = 0
        for which, src_ap, ntile, ng, tw in ((0, w1s_d, 8, 8, 256),
                                             (1, w2s_d, 8, 2, 1024)):
            rows = ng * 128
            wide = ng * tw
            ntp = tw // 128
            v_tp = 0
            for k in range(ntile):
                P.wait("sync", fa_free[0], fa_free[1])
                if which == 0 and k == 0:
                    P.wait("sync", "vec", v_vev[(7, 3)])
                vl = P.dma("sync", fa[:, :].rearrange("p (g c) -> p g c", c=tw),
                           src_ap[k * rows:(k + 1) * rows, :].rearrange(
                               "(g p) c -> p g c", p=128))
                P.wait("vector", vl)
                P.wait("vector", "vec", v_invbc)
                if which == 0 and k == 0:
                    P.wait("vector", toks_vst)
                    P.wait("vector", toks_kst)
                    P.wait("vector", "act", v_xn_last)
                P.wait("vector", tern_free[0], tern_free[1])
                m = which
                P.op("vector", lambda e, m=m: e.tensor_scalar(
                    out=fb[:, :], in0=fa[:, :], scalar1=inv_bc[:, m:m + 1],
                    scalar2=-HALFBIG, op0=ALU.mult, op1=ALU.add), inc="vec")
                P.wait("vector", "vec", C["vec"])
                P.op("vector", lambda e: e.tensor_scalar(
                    out=hbuf[:, H_TU:H_TU + 2048], in0=fb[:, :], scalar1=0.0,
                    scalar2=1.0, op0=ALU.max, op1=ALU.min), inc="vec")
                P.wait("vector", "vec", C["vec"])
                v_u2 = P.op("vector", lambda e, m=m: e.tensor_scalar(
                    out=fb[:, :], in0=fa[:, :],
                    scalar1=inv_bc[:, 2 + m:3 + m],
                    scalar2=-HALFBIG, op0=ALU.mult, op1=ALU.add), inc="vec")
                fa_free = ("vec", v_u2)
                P.wait("vector", "vec", C["vec"])
                P.op("vector", lambda e: e.tensor_scalar(
                    out=hbuf[:, H_TV:H_TV + 2048], in0=fb[:, :], scalar1=0.0,
                    scalar2=1.0, op0=ALU.max, op1=ALU.min), inc="vec")
                P.wait("vector", "vec", C["vec"])
                v_tt = P.op("vector", lambda e: e.tensor_tensor(
                    out=hbuf[:, H_TT:H_TT + 2048], in0=hbuf[:, H_TU:H_TU + 2048],
                    in1=hbuf[:, H_TV:H_TV + 2048], op=ALU.subtract), inc="vec")
                P.wait("tensor", "vec", v_tt)
                cw = 8192 if which == 0 else 2048
                for g in range(ng):
                    half = (k * ng + g) % 2
                    P.wait("tensor", pf_free[half][0], pf_free[half][1])
                    for j in range(ntp):
                        fn = lambda e, g=g, j=j, half=half, tw=tw: e.transpose(
                            P_f[:, half * 1024 + j * 128:
                                half * 1024 + (j + 1) * 128],
                            hbuf[:, H_TT + g * tw + j * 128:
                                 H_TT + g * tw + (j + 1) * 128], ident[:, :])
                        if j == ntp - 1:
                            v_tp = P.op("tensor", fn, inc="mm")
                        else:
                            P.op("tensor", fn)
                    kk = k * ng + g
                    P.wait("vector", "mm", v_tp)
                    if which == 1 and kk == 0:
                        P.wait("vector", toks_w1st)
                    o3 = hbuf[:, H_WST:H_WST + ntp * cw].rearrange(
                        "p (j c) -> p j c", c=cw)[:, :, kk * 128:kk * 128 + 128]
                    i3 = P_f[:, half * 1024: half * 1024 + ntp * 128].rearrange(
                        "p (j c) -> p j c", c=128)
                    pf_free[half] = ("vec", P.op(
                        "vector", lambda e, o3=o3, i3=i3: e.tensor_copy(o3, i3),
                        inc="vec"))
                tern_free = ("mm", v_tp)
            nrow = ntp
            cw = 8192 if which == 0 else 2048
            dst = d_w1t_in if which == 0 else d_w2t_in
            toks_std = []
            for j in range(nrow):
                P.wait("sync", "vec", pf_free[0][1])
                P.wait("sync", "vec", pf_free[1][1])
                toks_std.append(P.dma(
                    "sync", dst[j * 128:(j + 1) * 128, :],
                    hbuf[:, H_WST + j * cw:H_WST + (j + 1) * cw]))
            if which == 0:
                toks_w1st = toks_std
                P.wait("gpsimd", toks_w1st)
                v_agw1 = P.collective("cw1", "AllGather", ALU.bypass, GROUP_ALL,
                                      d_w1t_in, d_w1t)
            else:
                toks_w2st = toks_std
                P.wait("gpsimd", toks_w2st)
                v_agw2 = P.collective("cw2", "AllGather", ALU.bypass, GROUP_ALL,
                                      d_w2t_in, d_w2t)

        kvld = {}
        feats_evac = {}
        bc_copy = {}
        v_feats_evac_last = 0
        exp_free = ("mm", 0)

        for j in range(PAIRS):
            p = j % 2
            kb = hbuf[:, H_KB + p * 4096: H_KB + (p + 1) * 4096]
            vb = hbuf[:, H_VB + p * 4160: H_VB + (p + 1) * 4160]
            if j >= 2:
                P.wait("sync", "vec", feats_evac[(j - 2, 1)])
            else:
                P.wait("sync", "cck", v_agk)
                P.wait("sync", "ccv", v_agv)
                P.wait("sync", toks_w2st)
                P.wait("sync", "mm", tern_free[1])
            toks_kv = []
            for r in range(4):
                toks_kv.append(P.dma(
                    "sync", kb[:, r * 1024:(r + 1) * 1024],
                    d_k[r * 2048 + j * 128: r * 2048 + (j + 1) * 128, :]))
            for r in range(4):
                toks_kv.append(P.dma(
                    "sync", chunked(vb[:, r * 1040:(r + 1) * 1040], 130),
                    d_v[r * 1024:(r + 1) * 1024,
                        130 * j:130 * (j + 1)].rearrange(
                            "(c p) m -> p c m", p=128)))
            kvld[j] = toks_kv

            for qt in range(2):
                exp_ms = {}
                q0 = qt * 512
                expb = [hbuf[:, H_EA + par * 1024: H_EA + (par + 1) * 1024]
                        for par in range(2)]

                def mm2_pair(kc):
                    par = kc % 2
                    for (head, Pfe) in ((0, P_c), (1, P_d)):
                        cbase = kc * 130 + head * 65
                        fn = (lambda e, vb=vb, cbase=cbase, Pfe=Pfe,
                              par=par, head=head, kc=kc: e.matmul(
                                  Pfe[0:65, :],
                                  vb[:, cbase:cbase + 65],
                                  expb[par][:, head * 512:(head + 1) * 512],
                                  start=(kc == 0), stop=(kc == 31)))
                        yield fn

                for kc in range(32):
                    par = kc % 2
                    Psc = P_a if par == 0 else P_b
                    if kc == 0:
                        if qt == 0:
                            P.wait("tensor", kvld[j])
                            if j == 0:
                                P.wait("tensor", "act", v_qev)
                            if j >= 2:
                                P.wait("tensor", "vec", feats_evac[(j - 2, 1)])
                        if (j, qt) != (0, 0):
                            prev = (j, 0) if qt == 1 else (j - 1, 1)
                            P.wait("tensor", "vec", bc_copy.get(prev, 0))
                    else:
                        P.wait("tensor", "act", exp_ms.get(kc - 2, 0))
                        if kc == 2:
                            prev = (j, 0) if qt == 1 else (j - 1, 1)
                            if (j, qt) != (0, 0):
                                P.wait("tensor", "vec", feats_evac.get(prev, 0))
                        if kc >= 2:
                            for fn in mm2_pair(kc - 2):
                                P.op("tensor", fn)
                    v_g = 0
                    for head, rl, rh in ((0, 0, 64), (1, 64, 128)):
                        fn = (lambda e, kb=kb, kc=kc, Psc=Psc, rl=rl, rh=rh,
                              j=j, q0=q0, head=head: e.matmul(
                                  Psc[:, head * 512:(head + 1) * 512],
                                  kb[rl:rh, kc * 128:(kc + 1) * 128],
                                  qbuf[rl:rh, j * T + q0: j * T + q0 + 512],
                                  start=True, stop=True,
                                  tile_position=(rl, 0)))
                        if head == 1:
                            v_g = P.op("tensor", fn, inc="mm")
                        else:
                            P.op("tensor", fn)
                    P.wait("scalar", "mm", v_g)
                    exp_ms[kc] = P.op("scalar", lambda e, par=par, Psc=Psc:
                                      e.activation(out=expb[par][:, :],
                                                   in_=Psc[:, :], func=AF.Exp,
                                                   scale=SCALE), inc="act")
                P.wait("tensor", "act", exp_ms[31])
                v_feats = 0
                fns = list(mm2_pair(30)) + list(mm2_pair(31))
                for idx, fn in enumerate(fns):
                    if idx == len(fns) - 1:
                        v_feats = P.op("tensor", fn, inc="mm")
                    else:
                        P.op("tensor", fn)
                P.wait("vector", "mm", v_feats)
                P.op("vector", lambda e: e.reciprocal(
                    recA[0:1, :], P_c[64:65, :]), inc="vec")
                v_rec = P.op("vector", lambda e: e.reciprocal(
                    recB[0:1, :], P_d[64:65, :]), inc="vec")
                P.wait("tensor", "vec", v_rec)
                P.op("tensor", lambda e: e.matmul(
                    P_a[0:64, 0:512], ones64f[0:1, :], recA[0:1, :],
                    start=True, stop=True))
                v_bc = P.op("tensor", lambda e: e.matmul(
                    P_b[0:64, 0:512], ones64f[0:1, :], recB[0:1, :],
                    start=True, stop=True), inc="mm")
                P.wait("vector", "mm", v_bc)
                if j == 0 and qt == 0:
                    P.wait("vector", toks_w2st)
                fcol = j * T + q0
                P.op("vector", lambda e: e.tensor_copy(
                    bcbufA[:, :], P_a[0:64, 0:512]), inc="vec")
                v_bcc = P.op("vector", lambda e: e.tensor_copy(
                    bcbufB[:, :], P_b[0:64, 0:512]), inc="vec")
                P.op("vector", lambda e, fcol=fcol: e.tensor_tensor(
                    out=hbuf[0:64, H_FE + fcol:H_FE + fcol + 512],
                    in0=P_c[0:64, :], in1=bcbufA[:, :], op=ALU.mult), inc="vec")
                vfe = P.op("vector", lambda e, fcol=fcol: e.tensor_tensor(
                    out=hbuf[64:128, H_FE + fcol:H_FE + fcol + 512],
                    in0=P_d[0:64, :], in1=bcbufB[:, :], op=ALU.mult), inc="vec")
                bc_copy[(j, qt)] = v_bcc
                feats_evac[(j, qt)] = vfe
                v_feats_evac_last = vfe

        v_dbg_fe = None
        if DEBUG:
            P.wait("sync", "vec", v_feats_evac_last)
            v_dbg_fe = [P.dma("sync", o_fe[:, :], hbuf[:, H_FE:H_FE + NCH * T]),
                        P.dma("sync", o_q[:, :], qbuf[:, :])]

        wo_band_free = {0: ("mm", 0), 1: ("mm", 0)}
        wo_psum_free = {0: ("vec", v_feats_evac_last), 1: ("vec", v_feats_evac_last)}
        v_woev = {}
        v_womm_last = 0
        for jd in range(16):
            p = jd % 2
            wb = (wbufA if p == 0 else wbufB)[:, 0:2048]
            P.wait("gpsimd", wo_band_free[p][0], wo_band_free[p][1])
            vw = P.dma("gpsimd", chunked(wb, 128), band_ap(wo_d, jd))
            P.wait("tensor", vw)
            P.wait("tensor", wo_psum_free[p][0], wo_psum_free[p][1])
            if jd == 0:
                P.wait("tensor", "vec", v_feats_evac_last)
                if DEBUG:
                    P.wait("tensor", v_dbg_fe)
            P_x = P_a if p == 0 else P_b
            for hh in range(2):
                for i in range(16):
                    fe = H_FE + i * T + hh * 512
                    fn = lambda e, wb=wb, i=i, hh=hh, P_x=P_x, fe=fe: e.matmul(
                        P_x[:, hh * 512:(hh + 1) * 512],
                        wb[:, i * 128:(i + 1) * 128],
                        hbuf[:, fe:fe + 512],
                        start=(i == 0), stop=(i == 15))
                    if hh == 1 and i == 15:
                        v_womm = P.op("tensor", fn, inc="mm")
                    else:
                        P.op("tensor", fn)
            wo_band_free[p] = ("mm", v_womm)
            v_womm_last = v_womm
            P.wait("scalar", "mm", v_womm)
            vv = P.op("scalar", lambda e, P_x=P_x, jd=jd: e.activation(
                out=qbuf[:, jd * T:(jd + 1) * T], in_=P_x[:, :],
                func=AF.Identity, bias=s_bo[:, jd:jd + 1], scale=1.0), inc="act")
            v_woev[jd] = vv
            wo_psum_free[p] = ("act", vv)
        v_woev_last = v_woev[15]

        w1_free = {0: ("mm", 0), 1: ("mm", 0)}
        h_psum_free = {0: ("act", 0), 1: ("act", 0)}
        f2_psum_free = ("act", 0)
        v_fb_last = C["vec"]
        fa_slot_free = {sl: ("vec", v_fb_last) for sl in range(4)}
        fb_slot_free = {sl: ("vec", v_fb_last) for sl in range(4)}
        pbf_free = {0: ("mm", 0), 1: ("mm", 0)}
        pf2_free = {0: pf_free[0], 1: pf_free[1]}
        fout_free = {0: None, 1: None}
        w2_free = {0: ("mm", 0), 1: ("mm", 0)}
        v_out_dma = 0
        first_gelu = True
        for th in range(2):
            t0 = th * 512
            v_hmm = {}
            v_gelu = {}
            for sb_ in range(32):
                p = sb_ % 2
                wb = (wbufA if p == 0 else wbufB)[:, 0:4096]
                qeng = "sync" if sb_ % 2 == 0 else "gpsimd"
                P.wait(qeng, w1_free[p][0], w1_free[p][1])
                if th == 0 and sb_ < 2:
                    P.wait(qeng, "cw1", 1)
                    P.wait(qeng, "mm", v_womm_last)
                toks_w1l = [P.dma(
                    qeng,
                    wb.rearrange("p (i c) -> p i c", c=256),
                    d_w1t[:, sb_ * 256:(sb_ + 1) * 256].rearrange(
                        "(i p) c -> p i c", p=128))]
                for l in range(2):
                    bb = sb_ * 2 + l
                    pp = bb % 2
                    P_x = P_c if pp == 0 else P_d
                    P.wait("tensor", toks_w1l)
                    P.wait("tensor", h_psum_free[pp][0], h_psum_free[pp][1])
                    if th == 0 and bb == 0:
                        P.wait("tensor", "vec", v_feats_evac_last)
                    for i in range(16):
                        fn = (lambda e, wb=wb, i=i, l=l, P_x=P_x, t0=t0: e.matmul(
                            P_x[:, :],
                            wb[:, i * 256 + l * 128: i * 256 + (l + 1) * 128],
                            xnbuf[:, i * T + t0: i * T + t0 + 512],
                            start=(i == 0), stop=(i == 15)))
                        if i == 15:
                            v_hmm[bb] = P.op("tensor", fn, inc="mm")
                        else:
                            P.op("tensor", fn)
                    P.wait("scalar", "mm", v_hmm[bb])
                    if DEBUG and th == 1 and bb == 0:
                        P.wait("scalar", v_dbg_h)
                    if first_gelu:
                        P.wait("scalar", "mm", v_womm_last)
                        first_gelu = False
                    v_gelu[bb] = P.op("scalar", lambda e, P_x=P_x, bb=bb: e.activation(
                        out=hbuf[:, H_H + bb * 512:H_H + (bb + 1) * 512],
                        in_=P_x[:, :], func=AF.Gelu, bias=s_b1[:, bb:bb + 1],
                        scale=1.0), inc="act")
                    h_psum_free[pp] = ("act", v_gelu[bb])
                w1_free[p] = ("mm", v_hmm[sb_ * 2 + 1])
            v_h_done = v_gelu[63]
            if DEBUG and th == 0:
                P.wait("sync", "act", v_h_done)
                v_dbg_h = P.dma("sync", o_h[:, :], hbuf[:, H_H:H_H + 32768])
            f2slots = [P_a[:, 0:512], P_a[:, 512:1024],
                       P_b[:, 0:512], P_b[:, 512:1024]]
            for g in range(4):
                v_f2 = {}
                for fg in range(16):
                    p = fg % 2
                    w2b = (wbufA if p == 0 else wbufB)[:, 4096:6144]
                    P.wait("sync" if p == 0 else "gpsimd",
                           w2_free[p][0], w2_free[p][1])
                    if th == 0 and g == 0 and fg < 2:
                        P.wait("sync" if p == 0 else "gpsimd", "cw2", 1)
                    vld = P.dma(
                        "sync" if p == 0 else "gpsimd",
                        w2b.rearrange("p (f c) -> p f c", c=512),
                        d_w2t[fg * 512:(fg + 1) * 512,
                              g * 512:(g + 1) * 512].rearrange(
                                  "(f p) c -> p f c", p=128))
                    P.wait("tensor", vld)
                    if fg == 0:
                        P.wait("tensor", "act", v_h_done)
                        P.wait("tensor", f2_psum_free[0], f2_psum_free[1])
                        if th == 0 and g == 0:
                            P.wait("tensor", "act", v_woev_last)
                    for fl in range(4):
                        fbk = fg * 4 + fl
                        for sl in range(4):
                            fn = (lambda e, w2b=w2b, sl=sl, fl=fl, fbk=fbk:
                                  e.matmul(
                                      f2slots[sl],
                                      w2b[:, fl * 512 + sl * 128:
                                          fl * 512 + (sl + 1) * 128],
                                      hbuf[:, H_H + fbk * 512:
                                           H_H + (fbk + 1) * 512],
                                      start=(fbk == 0), stop=(fbk == 63)))
                            if sl == 3 and fl == 3:
                                v_f2[fg] = P.op("tensor", fn, inc="mm")
                            else:
                                P.op("tensor", fn)
                    w2_free[p] = ("mm", v_f2[fg])
                v_last_evac = 0
                for sl in range(4):
                    jd = g * 4 + sl
                    ppar = jd % 2
                    P.wait("scalar", "mm", v_f2[15])
                    P.wait("scalar", fa_slot_free[sl][0], fa_slot_free[sl][1])
                    v_silu = P.op("scalar", lambda e, sl=sl, jd=jd: e.activation(
                        out=fa[:, sl * 512:(sl + 1) * 512], in_=f2slots[sl],
                        func=AF.Silu, bias=s_b2[:, jd:jd + 1], scale=1.0),
                        inc="act")
                    v_last_evac = v_silu
                    P.wait("vector", "act", v_silu)
                    if th == 0 and g == 0:
                        P.wait("vector", "act", v_woev_last)
                    P.wait("vector", pbf_free[ppar][0], pbf_free[ppar][1])
                    v_p = P.op("vector", lambda e, sl=sl, jd=jd, t0=t0, ppar=ppar:
                               e.tensor_tensor(
                                   out=hbuf[:, H_PBF + ppar * 512:
                                            H_PBF + (ppar + 1) * 512],
                                   in0=fa[:, sl * 512:(sl + 1) * 512],
                                   in1=qbuf[:, jd * T + t0: jd * T + t0 + 512],
                                   op=ALU.mult), inc="vec")
                    fa_slot_free[sl] = ("vec", v_p)
                    P.wait("tensor", "vec", v_p)
                    P.wait("tensor", pf2_free[ppar][0], pf2_free[ppar][1])
                    v_tp5 = 0
                    for s4 in range(4):
                        fn = (lambda e, s4=s4, ppar=ppar: e.transpose(
                            P_f[:, ppar * 1024 + s4 * 128:
                                ppar * 1024 + (s4 + 1) * 128],
                            hbuf[:, H_PBF + ppar * 512 + s4 * 128:
                                 H_PBF + ppar * 512 + (s4 + 1) * 128],
                            ident[:, :]))
                        if s4 == 3:
                            v_tp5 = P.op("tensor", fn, inc="mm")
                        else:
                            P.op("tensor", fn)
                    pbf_free[ppar] = ("mm", v_tp5)
                    P.wait("sync", fb_slot_free[sl][0], fb_slot_free[sl][1])
                    vxl = P.dma("sync",
                                chunked(fb[:, sl * 512:(sl + 1) * 512], 128),
                                x_d[th * 512:(th + 1) * 512,
                                    jd * 128:(jd + 1) * 128].rearrange(
                                        "(s p) d -> p s d", p=128))
                    P.wait("vector", "mm", v_tp5)
                    P.wait("vector", vxl)
                    P.wait("vector", fout_free[ppar])
                    v_ot = P.op("vector", lambda e, sl=sl, ppar=ppar:
                                e.tensor_tensor(
                                    out=fout[:, ppar * 512:(ppar + 1) * 512],
                                    in0=P_f[:, ppar * 1024:ppar * 1024 + 512],
                                    in1=fb[:, sl * 512:(sl + 1) * 512],
                                    op=ALU.add), inc="vec")
                    pf2_free[ppar] = ("vec", v_ot)
                    fb_slot_free[sl] = ("vec", v_ot)
                    P.wait("sync", "vec", v_ot)
                    v_out_dma = P.dma(
                        "sync",
                        out_d[th * 512:(th + 1) * 512,
                              jd * 128:(jd + 1) * 128].rearrange(
                                  "(s p) d -> p s d", p=128),
                        chunked(fout[:, ppar * 512:(ppar + 1) * 512], 128))
                    fout_free[ppar] = v_out_dma
                f2_psum_free = ("act", v_last_evac)

        if DEBUG:
            P.wait("sync", "vec", C["vec"])
            P.wait("sync", "act", C["act"])
            P.dma("sync", o_xn[:, :], xnbuf[:, :])
            P.dma("sync", o_st[:, :], qbuf[:, :])
            P.dma("sync", o_sum[:, :], ssmall[0:1, 2:4])
            P.dma("sync", o_kg[:, :], d_k[:, :])
            P.dma("sync", o_vg[:, :], d_v[:, :])
            for r in range(16):
                P.dma("sync", o_w1t[r * 128:(r + 1) * 128, :],
                      d_w1t[r * 128:(r + 1) * 128, :])
            for r in range(64):
                P.dma("sync", o_w2t[r * 128:(r + 1) * 128, :],
                      d_w2t[r * 128:(r + 1) * 128, :])

        for s in SQ:
            P.wait("sync", s, C[s])

        with nc.Block() as block:
            @block.sync
            def _(e):
                for fn in P.items["sync"]:
                    fn(e)

            @block.gpsimd
            def _(e):
                for fn in P.items["gpsimd"]:
                    fn(e)

            @block.tensor
            def _(e):
                for fn in P.items["tensor"]:
                    fn(e)

            @block.vector
            def _(e):
                for fn in P.items["vector"]:
                    fn(e)

            @block.scalar
            def _(e):
                for fn in P.items["scalar"]:
                    fn(e)

    return nc


_NC_CACHE = None


def kernel(L0_attn_x, L0_attn_Wq, L0_attn_bq, L0_attn_Wk, L0_attn_bk,
           L0_attn_Wv, L0_attn_bv, L0_attn_Wo, L0_attn_bo,
           L0_ffn_w1, L0_ffn_b1, L0_ffn_w2, L0_ffn_b2):
    global _NC_CACHE
    x = np.ascontiguousarray(np.asarray(L0_attn_x, dtype=np.float32))
    wq = np.ascontiguousarray(np.asarray(L0_attn_Wq, dtype=np.float32))
    wk = np.ascontiguousarray(np.asarray(L0_attn_Wk, dtype=np.float32))
    wv = np.ascontiguousarray(np.asarray(L0_attn_Wv, dtype=np.float32))
    wo = np.ascontiguousarray(np.asarray(L0_attn_Wo, dtype=np.float32))
    w1 = np.ascontiguousarray(np.asarray(L0_ffn_w1, dtype=np.float32))
    w2 = np.ascontiguousarray(np.asarray(L0_ffn_w2, dtype=np.float32))

    def fold(b, n):
        return np.ascontiguousarray(
            np.asarray(b, dtype=np.float32).reshape(n, 128).T)

    bq2 = fold(L0_attn_bq, 16)
    bk2 = fold(L0_attn_bk, 16)
    bo2 = fold(L0_attn_bo, 16)
    b12 = fold(L0_ffn_b1, 64)
    b22 = fold(L0_ffn_b2, 16)
    bvr = np.ascontiguousarray(
        np.asarray(L0_attn_bv, dtype=np.float32).reshape(1, D))

    if _NC_CACHE is None:
        _NC_CACHE = build_program()
    nc = _NC_CACHE

    in_maps = []
    for c in range(8):
        b, sl = c // 4, c % 4
        in_maps.append({
            "x": np.ascontiguousarray(x[b, sl * T:(sl + 1) * T, :]),
            "wq": wq, "wk": wk, "wv": wv, "wo": wo,
            "bq": bq2, "bk": bk2, "bo": bo2, "bv_row": bvr,
            "b1": b12, "b2": b22,
            "w1s": np.ascontiguousarray(w1[:, c * 256:(c + 1) * 256]),
            "w2s": np.ascontiguousarray(w2[:, c * 1024:(c + 1) * 1024]),
        })
    res = run_bass_kernel_spmd(nc, in_maps, core_ids=list(range(8)))
    out = np.empty((2, 4096, D), dtype=np.float32)
    for c in range(8):
        b, sl = c // 4, c % 4
        out[b, sl * T:(sl + 1) * T, :] = res.results[c]["out"]
    return out
